# version 01
# baseline (speedup 1.0000x reference)
import numpy as np
import ml_dtypes
from contextlib import ExitStack

import concourse.bass as bass
import concourse.mybir as mybir
from concourse.alu_op_type import AluOpType as OP
from concourse.bass_utils import run_bass_kernel_spmd

F32 = mybir.dt.float32
BF16 = mybir.dt.bfloat16
AF = mybir.ActivationFunctionType
AX = mybir.AxisListType.X

BC = 2
L = 2048
DM = 1024
DI = 2048
NS = 16
DTR = 64
NE = 8
JT = DM // 128
IT = DI // 128
LC = 512
NCH = L // LC
EPS = 1e-5
BIG = 1e30


class S:

    def __init__(self, ctx, nc, name):
        self.h = ctx.enter_context(nc.semaphore(name))
        self.n = 0

    def inc(self, instr, k=1):
        instr.then_inc(self.h, k)
        self.n += k
        return self.n


class Guard:

    def __init__(self):
        self.edges = []

    def set(self, sem, n):
        self.edges = [(sem.h, n)]

    def add(self, sem, n):
        self.edges.append((sem.h, n))

    def wait(self, eng):
        for h, n in self.edges:
            if n > 0:
                eng.wait_ge(h, n)
        self.edges = []


class G:
    pass


def emit_barrier(g):
    snaps = [(s.h, s.n) for s in [g.st, g.sv, g.ss, g.sp, g.sy, g.sg] if s.n > 0]

    for name in ["tensor", "vector", "scalar", "sync", "gpsimd"]:
        def body(eng):
            for h, n in snaps:
                eng.wait_ge(h, n)
        getattr(g.block, name)(body)


def build(dbg=(), phases=("p1a", "p1b", "p2", "p1c", "p3")):
    nc = bass.Bass("TRN2", target_bir_lowering=False)
    g = G()
    g.nc = nc

    def dram(name, shape, dtype, kind):
        if kind == "Internal" and name in dbg:
            kind = "ExternalOutput"
        return nc.dram_tensor(name, shape, dtype, kind=kind).ap()

    IN = "ExternalInput"
    g.xt = dram("xt", [BC, DM, L], F32, IN)
    g.w_inproj = dram("w_inproj", [DM, 2 * DI], F32, IN)
    g.w_xproj = dram("w_xproj", [DI, 96], F32, IN)
    g.w_dtw = dram("w_dtw", [DTR, DI], F32, IN)
    g.w_outw = dram("w_outw", [DI, DM], F32, IN)
    g.w_ffg = dram("w_ffg", [DM, 2 * DM], F32, IN)
    g.w_ffp = dram("w_ffp", [DM, DM], F32, IN)
    g.w_router = dram("w_router", [DM, NE], F32, IN)
    g.w_sg = dram("w_sg", [DM, DI], F32, IN)
    g.w_su = dram("w_su", [DM, DI], F32, IN)
    g.w_sd = dram("w_sd", [DI, DM], F32, IN)
    g.w_eg = dram("w_eg", [NE, DM, DI], F32, IN)
    g.w_eu = dram("w_eu", [NE, DM, DI], F32, IN)
    g.w_ed = dram("w_ed", [NE, DI, DM], F32, IN)
    g.v_n1g = dram("v_n1g", [128, JT], F32, IN)
    g.v_n1b = dram("v_n1b", [128, JT], F32, IN)
    g.v_n2g = dram("v_n2g", [128, JT], F32, IN)
    g.v_n2b = dram("v_n2b", [128, JT], F32, IN)
    g.v_convw = dram("v_convw", [128, IT * 4], F32, IN)
    g.v_convb = dram("v_convb", [128, IT], F32, IN)
    g.v_dtb = dram("v_dtb", [128, IT], F32, IN)
    g.v_D = dram("v_D", [128, IT], F32, IN)
    g.v_ffgb = dram("v_ffgb", [128, 2 * JT], F32, IN)
    g.v_ffpb = dram("v_ffpb", [128, JT], F32, IN)
    g.v_Alog = dram("v_Alog", [128, IT * NS], F32, IN)
    g.c_id = dram("c_id", [128, 128], F32, IN)
    g.c_idf = dram("c_idf", [128, 128], F32, IN)
    g.c_oh16 = dram("c_oh16", [16, 16 * 128], F32, IN)
    g.c_oh8 = dram("c_oh8", [8, 8 * 128], F32, IN)

    g.out_cm = dram("out_cm", [BC, DM, L], F32, "ExternalOutput")
    g.h_xn = nc.dram_tensor("h_xn", [BC, DM, L], BF16, kind="Internal").ap()
    g.h_xc = nc.dram_tensor("h_xc", [BC, DI, L], BF16, kind="Internal").ap()
    g.h_sz = nc.dram_tensor("h_sz", [BC, DI, L], BF16, kind="Internal").ap()
    g.h_yraw = nc.dram_tensor("h_yraw", [BC, DI, L], BF16, kind="Internal").ap()
    g.h_s = dram("h_s", [BC, DM, L], F32, "Internal")
    g.h_g = nc.dram_tensor("h_g", [BC, DM, L], BF16, kind="Internal").ap()
    g.h_rows = dram("h_rows", [2, BC * L], F32, "Internal")
    g.h_comb = nc.dram_tensor("h_comb", [4, 8, 1024], BF16, kind="Internal").ap()
    g.dbg_mirrors = []
    for nm in dbg:
        srcap = getattr(g, nm, None)
        if srcap is not None and srcap.tensor.dtype == BF16:
            mir = nc.dram_tensor("dbgo_" + nm, list(srcap.shape), F32,
                                 kind="ExternalOutput").ap()
            g.dbg_mirrors.append((srcap, mir))

    with ExitStack() as ctx:
        E = ctx.enter_context
        g.st = S(ctx, nc, "st")
        g.sv = S(ctx, nc, "sv")
        g.ss = S(ctx, nc, "ss")
        g.sp = S(ctx, nc, "sp")
        g.sy = S(ctx, nc, "sy")
        g.sg = S(ctx, nc, "sg")

        g.pb = [E(nc.psum_tensor(f"pb{i}", [128, 512], F32)) for i in range(7)]
        g.pbT = E(nc.psum_tensor("pbT", [128, 256], BF16))

        g.ones_f = E(nc.sbuf_tensor("ones_f", [128, 1], F32))
        g.ones1_f = E(nc.sbuf_tensor("ones1_f", [1, 128], F32))
        g.zeros3 = E(nc.sbuf_tensor("zeros3", [128, 4], BF16))
        g.cEPS = E(nc.sbuf_tensor("cEPS", [128, 1], F32))
        g.c1 = E(nc.sbuf_tensor("c1", [128, 1], F32))
        g.id_sb = E(nc.sbuf_tensor("id_sb", [128, 128], BF16))
        g.idf_sb = E(nc.sbuf_tensor("idf_sb", [128, 128], F32))
        g.oh8_sb = E(nc.sbuf_tensor("oh8_sb", [8, 8 * 128], BF16))
        g.n1g = E(nc.sbuf_tensor("n1g", [128, JT], F32))
        g.n1b = E(nc.sbuf_tensor("n1b", [128, JT], F32))
        g.n2g = E(nc.sbuf_tensor("n2g", [128, JT], F32))
        g.n2b = E(nc.sbuf_tensor("n2b", [128, JT], F32))
        g.convw = E(nc.sbuf_tensor("convw", [128, IT * 4], F32))
        g.convb = E(nc.sbuf_tensor("convb", [128, IT], F32))
        g.dtb = E(nc.sbuf_tensor("dtb", [128, IT], F32))
        g.Dv = E(nc.sbuf_tensor("Dv", [128, IT], F32))
        g.ffgb = E(nc.sbuf_tensor("ffgb", [128, 2 * JT], F32))
        g.ffpb = E(nc.sbuf_tensor("ffpb", [128, JT], F32))
        g.A_sb = E(nc.sbuf_tensor("A_sb", [128, IT * NS], F32))
        g.rt_sb = E(nc.sbuf_tensor("rt_sb", [128, JT * NE], F32))
        g.dt_sb = E(nc.sbuf_tensor("dt_sb", [DTR, BC * L], BF16))
        g.Brow = E(nc.sbuf_tensor("Brow", [NS, BC * L], BF16))
        g.Crow = E(nc.sbuf_tensor("Crow", [NS, BC * L], BF16))
        g.hstate = E(nc.sbuf_tensor("hstate", [128, IT * NS], F32))

        g.block = E(nc.Block())
        b = g.block

        def _setup_dma(sync):
            for dst, src in [(g.n1g, g.v_n1g), (g.n1b, g.v_n1b), (g.n2g, g.v_n2g),
                             (g.n2b, g.v_n2b), (g.convw, g.v_convw), (g.convb, g.v_convb),
                             (g.dtb, g.v_dtb), (g.Dv, g.v_D), (g.ffgb, g.v_ffgb),
                             (g.ffpb, g.v_ffpb), (g.idf_sb, g.c_idf),
                             (g.A_sb, g.v_Alog)]:
                g.sy.inc(sync.dma_start(out=dst[:, :], in_=src), 16)
            g.sy.inc(sync.dma_start(
                out=g.rt_sb[:, :].rearrange("p (k e) -> p k e", e=NE),
                in_=g.w_router.rearrange("(k p) e -> p k e", p=128)), 16)
        b.sync(_setup_dma)
        sd_n = g.sy.n

        def _setup_gdma(gq):
            g.sg.inc(gq.dma_start(out=g.id_sb[:, :], in_=g.c_id), 16)
            g.sg.inc(gq.dma_start(out=g.oh8_sb[:, :], in_=g.c_oh8), 16)
        b.gpsimd(_setup_gdma)
        sg_n = g.sg.n

        def _setup_v(v):
            v.wait_ge(g.sy.h, sd_n)
            v.wait_ge(g.sg.h, sg_n)
            v.memset(g.ones_f[:, :], 1.0)
            v.memset(g.ones1_f[:, :], 1.0)
            v.memset(g.zeros3[:, :], 0.0)
            v.memset(g.cEPS[:, :], EPS)
            v.memset(g.c1[:, :], 1.0)
            g.sv.inc(v.memset(g.hstate[:, :], 0.0))
        b.vector(_setup_v)

        def _setup_s(s):
            s.wait_ge(g.sy.h, sd_n)
            g.ss.inc(s.activation(out=g.A_sb[:, :], in_=g.A_sb[:, :], func=AF.Exp))
        b.scalar(_setup_s)
        ssn = g.ss.n

        def _setup_v2(v):
            v.wait_ge(g.ss.h, ssn)
            g.sv.inc(v.tensor_scalar(out=g.A_sb[:, :], in0=g.A_sb[:, :], scalar1=-1.0,
                                     scalar2=None, op0=OP.mult))
        b.vector(_setup_v2)
        emit_barrier(g)

        if "p1a" in phases:
            emit_p1a(g)
            emit_barrier(g)
        if "p1b" in phases:
            emit_p1b(g)
            emit_barrier(g)
        if "p2" in phases:
            emit_p2(g)
            emit_barrier(g)
        if "p1c" in phases:
            emit_p1c(g)
            emit_barrier(g)
        if "p3" in phases:
            emit_p3(g)

        if g.dbg_mirrors:
            emit_barrier(g)
            emit_dbg_dump(g)

        fin = g.sy.n

        def _fin(sync):
            sync.wait_ge(g.sy.h, fin)
        b.sync(_fin)

    return nc


def emit_dbg_dump(g):
    nc, b = g.nc, g.block
    with ExitStack() as c2:
        E = c2.enter_context
        tb = E(nc.sbuf_tensor("dbg_tb", [128, 2048], BF16))
        tf = E(nc.sbuf_tensor("dbg_tf", [128, 2048], F32))
        prev_cp = 0
        for src, dst in g.dbg_mirrors:
            B_, D_, L_ = src.shape
            PP = min(128, D_)
            for b_ in range(B_):
                for n_ in range(max(1, D_ // 128)):
                    def _in(sync, src=src, b_=b_, n_=n_, prev_cp=prev_cp):
                        if prev_cp:
                            sync.wait_ge(g.ss.h, prev_cp)
                        g.sy.inc(sync.dma_start(
                            out=tb[0:PP, 0:L_],
                            in_=src[b_][n_ * PP:(n_ + 1) * PP, :]), 16)
                    b.sync(_in)
                    lin = g.sy.n

                    def _cp(s, lin=lin):
                        s.wait_ge(g.sy.h, lin)
                        g.ss.inc(s.copy(out=tf[0:PP, 0:L_], in_=tb[0:PP, 0:L_]))
                    b.scalar(_cp)
                    cpn = g.ss.n
                    prev_cp = cpn

                    def _out(sync, dst=dst, b_=b_, n_=n_, cpn=cpn):
                        sync.wait_ge(g.ss.h, cpn)
                        g.sy.inc(sync.dma_start(
                            out=dst[b_][n_ * PP:(n_ + 1) * PP, :],
                            in_=tf[0:PP, 0:L_]), 16)
                    b.sync(_out)


def emit_p1a(g):
    nc, b = g.nc, g.block
    with ExitStack() as c2:
        E = c2.enter_context
        inproj = E(nc.sbuf_tensor("inproj_sl", [128, JT * 2 * DI], BF16))
        xproj = E(nc.sbuf_tensor("xproj_sl", [128, IT * 96], BF16))
        x_sb = E(nc.sbuf_tensor("x_sb", [128, JT * LC], F32))
        bc_rows = E(nc.sbuf_tensor("bc_rows", [128, 4 * LC], F32))
        xn_sb = E(nc.sbuf_tensor("xn_sb", [128, JT * LC], BF16))
        ln1_sb = E(nc.sbuf_tensor("ln1_sb", [128, JT * LC], BF16))
        xin = E(nc.sbuf_tensor("xin", [128, IT * (LC + 3)], BF16))
        sz_sb = E(nc.sbuf_tensor("sz_sb", [128, IT * LC], BF16))
        xc_sb = E(nc.sbuf_tensor("xc_sb", [128, IT * LC], BF16))
        sq_sb = [E(nc.sbuf_tensor(f"sq_sb{i}", [128, LC], F32)) for i in range(2)]
        tmpj = E(nc.sbuf_tensor("tmpj", [128, LC], F32))
        cv = [E(nc.sbuf_tensor(f"cv{i}", [128, LC], F32)) for i in range(2)]
        rtmp = [E(nc.sbuf_tensor(f"rtmp{i}", [1, LC], F32)) for i in range(4)]

        def _wload(gq):
            g.sg.inc(gq.dma_start(
                out=inproj[:, :].rearrange("p (k m) -> p k m", m=2 * DI),
                in_=g.w_inproj.rearrange("(k p) m -> p k m", p=128)), 16)
            g.sg.inc(gq.dma_start(
                out=xproj[:, :].rearrange("p (k m) -> p k m", m=96),
                in_=g.w_xproj.rearrange("(k p) m -> p k m", p=128)), 16)
        b.gpsimd(_wload)
        wl_n = g.sg.n

        chunks = [(bb, cc) for bb in range(BC) for cc in range(NCH)]
        xload = []
        x_free = Guard()
        bank_g = {i: Guard() for i in range(7)}
        hist_g = Guard()
        spill_prev = 0
        xn_st_prev = 0
        sz_st_prev = 0
        xc_st_prev = 0

        for ci, (bb, cc) in enumerate(chunks):
            buf = x_sb
            col = cc * LC

            def _ld(sync, ci=ci, bb=bb, col=col, buf=buf):
                x_free.wait(sync)
                g.sy.inc(sync.dma_start(
                    out=buf[:, :].rearrange("p (j t) -> p j t", t=LC),
                    in_=g.xt[bb][:, col:col + LC].rearrange("(j p) t -> p j t", p=128)), 16)
            b.sync(_ld)
            xload.append(g.sy.n)

            sqv = []
            sqc = []
            for j in range(JT):
                def _sqv(v, j=j, ci=ci, buf=buf):
                    if j == 0:
                        v.wait_ge(g.sy.h, xload[ci])
                    if j >= 2:
                        v.wait_ge(g.st.h, sqc[j - 2])
                    sl = buf[:, j * LC:(j + 1) * LC]
                    sqv.append(g.sv.inc(v.tensor_tensor(
                        out=sq_sb[j % 2][:, :], in0=sl, in1=sl, op=OP.mult)))
                b.vector(_sqv)

                def _stats_j(t, j=j, ci=ci, buf=buf):
                    if j == 0:
                        bank_g[0].wait(t)
                        bank_g[1].wait(t)
                        t.wait_ge(g.sy.h, xload[ci])
                    t.matmul(g.pb[0][0:1, 0:LC], g.ones_f[:, :], buf[:, j * LC:(j + 1) * LC],
                             start=(j == 0), stop=(j == JT - 1))
                    t.wait_ge(g.sv.h, sqv[j])
                    mm = t.matmul(g.pb[1][0:1, 0:LC], g.ones_f[:, :], sq_sb[j % 2][:, :],
                                  start=(j == 0), stop=(j == JT - 1))
                    sqc.append(g.st.inc(mm))
                b.tensor(_stats_j)
            stats_mm = g.st.n

            mu0, msq, var0, vpe = rtmp
            RS0, RS1, RS2, RS3 = 0, LC, 2 * LC, 3 * LC

            def _rowsA(v):
                v.wait_ge(g.st.h, stats_mm)
                v.tensor_scalar(out=mu0[:, :], in0=g.pb[0][0:1, 0:LC], scalar1=1.0 / DM,
                                scalar2=None, op0=OP.mult)
                v.tensor_scalar(out=msq[:, :], in0=g.pb[1][0:1, 0:LC], scalar1=1.0 / DM,
                                scalar2=None, op0=OP.mult)
                v.tensor_tensor(out=var0[:, :], in0=mu0[:, :], in1=mu0[:, :], op=OP.mult)
                v.tensor_tensor(out=var0[:, :], in0=msq[:, :], in1=var0[:, :], op=OP.subtract)
                v.tensor_scalar(out=vpe[:, :], in0=var0[:, :], scalar1=EPS, scalar2=None,
                                op0=OP.add)
                v.reciprocal(out=msq[:, :], in_=vpe[:, :])
                g.sv.inc(v.tensor_tensor(out=var0[:, :], in0=var0[:, :], in1=msq[:, :],
                                         op=OP.mult))
            b.vector(_rowsA)
            rA = g.sv.n
            bank_g[0].set(g.sv, rA)
            bank_g[1].set(g.sv, rA)

            def _rowsB(s, spill_prev=spill_prev):
                s.wait_ge(g.sv.h, rA)
                if spill_prev:
                    s.wait_ge(g.sy.h, spill_prev)
                s.activation(out=vpe[:, :], in_=vpe[:, :], func=AF.Ln)
                s.activation(out=bc_rows[0:1, RS0:RS0 + LC], in_=vpe[:, :], func=AF.Exp,
                             scale=-0.5)
                s.activation(out=var0[:, :], in_=var0[:, :], func=AF.Ln, bias=g.cEPS[0:1, 0:1])
                g.ss.inc(s.activation(out=var0[:, :], in_=var0[:, :], func=AF.Exp, scale=-0.5))
            b.scalar(_rowsB)
            rB = g.ss.n

            def _rowsC(v):
                v.wait_ge(g.ss.h, rB)
                v.tensor_tensor(out=bc_rows[0:1, RS1:RS1 + LC], in0=mu0[:, :],
                                in1=bc_rows[0:1, RS0:RS0 + LC], op=OP.mult)
                v.tensor_tensor(out=bc_rows[0:1, RS2:RS2 + LC],
                                in0=bc_rows[0:1, RS0:RS0 + LC], in1=var0[:, :], op=OP.mult)
                g.sv.inc(v.tensor_tensor(out=bc_rows[0:1, RS3:RS3 + LC],
                                         in0=bc_rows[0:1, RS1:RS1 + LC], in1=var0[:, :],
                                         op=OP.mult))
            b.vector(_rowsC)
            rC = g.sv.n

            def _rowspill(sync, bb=bb, col=col):
                sync.wait_ge(g.sv.h, rC)
                off = bb * L + col
                g.sy.inc(sync.dma_start(out=g.h_rows[0:1, off:off + LC],
                                        in_=bc_rows[0:1, RS0:RS0 + LC]), 16)
                g.sy.inc(sync.dma_start(out=g.h_rows[1:2, off:off + LC],
                                        in_=bc_rows[0:1, RS1:RS1 + LC]), 16)
            b.sync(_rowspill)
            spill_n = g.sy.n
            spill_prev = spill_n

            bc_s = []

            for q in range(4):
                def _bc_t(t, q=q):
                    if q == 0:
                        t.wait_ge(g.sv.h, rC)
                        t.wait_ge(g.sy.h, spill_n)
                    bank_g[2].wait(t)
                    mm = t.matmul(g.pb[2][:, 0:LC], g.ones1_f[:, :],
                                  bc_rows[0:1, q * LC:(q + 1) * LC], start=True, stop=True)
                    bc_t_n.append(g.st.inc(mm))
                bc_t_n = []
                b.tensor(_bc_t)

                def _bc_s(s, q=q):
                    s.wait_ge(g.st.h, bc_t_n[-1])
                    ev = s.copy(out=bc_rows[:, q * LC:(q + 1) * LC], in_=g.pb[2][:, 0:LC])
                    bc_s.append(g.ss.inc(ev))
                b.scalar(_bc_s)
                bank_g[2].set(g.ss, bc_s[-1])
            bc_n = bc_s[-1]

            def _xnl(v, ci=ci, buf=buf, xn_st_prev=xn_st_prev):
                v.wait_ge(g.ss.h, bc_n)
                if xn_st_prev:
                    v.wait_ge(g.sy.h, xn_st_prev)
                last = None
                for j in range(JT):
                    sl = buf[:, j * LC:(j + 1) * LC]
                    v.tensor_tensor(out=tmpj[:, :], in0=sl, in1=bc_rows[:, 0:LC], op=OP.mult)
                    v.tensor_tensor(out=xn_sb[:, j * LC:(j + 1) * LC], in0=tmpj[:, :],
                                    in1=bc_rows[:, LC:2 * LC], op=OP.subtract)
                    v.tensor_tensor(out=tmpj[:, :], in0=sl, in1=bc_rows[:, 2 * LC:3 * LC],
                                    op=OP.mult)
                    v.tensor_tensor(out=tmpj[:, :], in0=tmpj[:, :],
                                    in1=bc_rows[:, 3 * LC:4 * LC], op=OP.subtract)
                    last = v.tensor_scalar(out=ln1_sb[:, j * LC:(j + 1) * LC], in0=tmpj[:, :],
                                           scalar1=g.n1g[:, j:j + 1], scalar2=g.n1b[:, j:j + 1],
                                           op0=OP.mult, op1=OP.add)
                g.sv.inc(last)
            b.vector(_xnl)
            xnl_n = g.sv.n
            x_free.set(g.sv, xnl_n)

            def _xn_store(sync, bb=bb, col=col):
                sync.wait_ge(g.sv.h, xnl_n)
                g.sy.inc(sync.dma_start(
                    out=g.h_xn[bb][:, col:col + LC].rearrange("(j p) t -> p j t", p=128),
                    in_=xn_sb[:, :].rearrange("p (j t) -> p j t", t=LC)), 16)
            b.sync(_xn_store)
            xn_st = g.sy.n
            xn_st_prev = xn_st

            def _hist(s, cc=cc):
                hist_g.wait(s)
                last = None
                for it_ in range(IT):
                    base = it_ * (LC + 3)
                    if cc == 0:
                        last = s.copy(out=xin[:, base:base + 3], in_=g.zeros3[:, 0:3])
                    else:
                        last = s.copy(out=xin[:, base:base + 3],
                                      in_=xin[:, base + LC:base + LC + 3])
                g.ss.inc(last)
            b.scalar(_hist)
            hist_n = g.ss.n

            ipm = []
            ipe = []

            def _ip_t(t, ci=ci):
                t.wait_ge(g.sv.h, xnl_n)
                if ci == 0:
                    t.wait_ge(g.sg.h, wl_n)
                for m in range(2 * IT):
                    bank = 3 + (m % 2)
                    if m >= 2:
                        t.wait_ge(g.ss.h, ipe[m - 2])
                    else:
                        bank_g[bank].wait(t)
                    mm = None
                    for k in range(JT):
                        mm = t.matmul(g.pb[bank][:, 0:LC],
                                      inproj[:, k * 2 * DI + m * 128:k * 2 * DI + (m + 1) * 128],
                                      ln1_sb[:, k * LC:(k + 1) * LC],
                                      start=(k == 0), stop=(k == JT - 1))
                    ipm.append(g.st.inc(mm))
                    _ip_ev(m)

            def _ip_ev(m):
                def _ev(s, m=m, sz_st_prev=sz_st_prev):
                    bank = 3 + (m % 2)
                    s.wait_ge(g.st.h, ipm[m])
                    if m == IT and sz_st_prev:
                        s.wait_ge(g.sy.h, sz_st_prev)
                    if m < IT:
                        base = m * (LC + 3)
                        ev = s.copy(out=xin[:, base + 3:base + 3 + LC], in_=g.pb[bank][:, 0:LC])
                    else:
                        mz = m - IT
                        ev = s.activation(out=sz_sb[:, mz * LC:(mz + 1) * LC],
                                          in_=g.pb[bank][:, 0:LC], func=AF.Silu)
                    ipe.append(g.ss.inc(ev))
                b.scalar(_ev)
            b.tensor(_ip_t)
            bank_g[3].set(g.ss, ipe[-2])
            bank_g[4].set(g.ss, ipe[-1])
            ip_all = ipe[-1]

            def _sz_store(sync, bb=bb, col=col):
                sync.wait_ge(g.ss.h, ip_all)
                g.sy.inc(sync.dma_start(
                    out=g.h_sz[bb][:, col:col + LC].rearrange("(i p) t -> p i t", p=128),
                    in_=sz_sb[:, :].rearrange("p (i t) -> p i t", t=LC)), 16)
            b.sync(_sz_store)
            sz_st_prev = g.sy.n

            convd = []
            silud = []

            def _conv(v):
                for it_ in range(IT):
                    base = it_ * (LC + 3)
                    v.wait_ge(g.ss.h, ipe[it_])
                    v.wait_ge(g.ss.h, hist_n)
                    if it_ >= 2:
                        v.wait_ge(g.ss.h, silud[it_ - 2])
                    cb_ = cv[it_ % 2]
                    v.tensor_scalar(out=cb_[:, :], in0=xin[:, base:base + LC],
                                    scalar1=g.convw[:, it_ * 4:it_ * 4 + 1], scalar2=None,
                                    op0=OP.mult)
                    last = None
                    for k in range(1, 4):
                        last = v.scalar_tensor_tensor(
                            out=cb_[:, :], in0=xin[:, base + k:base + k + LC],
                            scalar=g.convw[:, it_ * 4 + k:it_ * 4 + k + 1],
                            in1=cb_[:, :], op0=OP.mult, op1=OP.add)
                    convd.append(g.sv.inc(last))
                    _silu(it_)

            def _silu(it_):
                def _sl(s, it_=it_, xc_st_prev=xc_st_prev):
                    s.wait_ge(g.sv.h, convd[it_])
                    if it_ == 0 and xc_st_prev:
                        s.wait_ge(g.sy.h, xc_st_prev)
                    sl = s.activation(out=xc_sb[:, it_ * LC:(it_ + 1) * LC],
                                      in_=cv[it_ % 2][:, :], func=AF.Silu,
                                      bias=g.convb[:, it_:it_ + 1])
                    silud.append(g.ss.inc(sl))
                b.scalar(_sl)
            b.vector(_conv)
            hist_g.set(g.sv, convd[-1])
            xc_n = silud[-1]

            def _xc_store(sync, bb=bb, col=col):
                sync.wait_ge(g.ss.h, xc_n)
                g.sy.inc(sync.dma_start(
                    out=g.h_xc[bb][:, col:col + LC].rearrange("(i p) t -> p i t", p=128),
                    in_=xc_sb[:, :].rearrange("p (i t) -> p i t", t=LC)), 16)
            b.sync(_xc_store)
            xc_st_prev = g.sy.n

            xpm = []
            xpe = []

            def _xp_t(t):
                t.wait_ge(g.ss.h, xc_n)
                for gi, (bank, off, w) in enumerate([(5, 0, DTR), (6, DTR, NS), (0, DTR + NS, NS)]):
                    bank_g[bank].wait(t)
                    mm = None
                    for k in range(IT):
                        mm = t.matmul(g.pb[bank][0:w, 0:LC],
                                      xproj[:, k * 96 + off:k * 96 + off + w],
                                      xc_sb[:, k * LC:(k + 1) * LC],
                                      start=(k == 0), stop=(k == IT - 1))
                    xpm.append(g.st.inc(mm))
                    _xp_ev(gi, bank, w)

            def _xp_ev(gi, bank, w, bb=bb, col=col):
                def _ev(s, gi=gi, bank=bank, w=w):
                    s.wait_ge(g.st.h, xpm[gi])
                    dcol = bb * L + col
                    dst = [g.dt_sb, g.Brow, g.Crow][gi]
                    ev = s.copy(out=dst[0:w, dcol:dcol + LC], in_=g.pb[bank][0:w, 0:LC])
                    xpe.append(g.ss.inc(ev))
                b.scalar(_ev)
            b.tensor(_xp_t)
            for gi, bank in enumerate([5, 6, 0]):
                bank_g[bank].set(g.ss, xpe[gi])
            x_free.add(g.st, stats_mm)


def emit_p1b(g):
    nc, b = g.nc, g.block
    with ExitStack() as c2:
        E = c2.enter_context
        dtw = E(nc.sbuf_tensor("dtw_sl", [DTR, DI], BF16))
        oh16_sb = E(nc.sbuf_tensor("oh16_sb", [16, 16 * 128], BF16))
        xc2 = [E(nc.sbuf_tensor(f"xc2_{i}", [128, IT * LC], BF16)) for i in range(2)]
        delta_all = E(nc.sbuf_tensor("delta_all", [128, IT * LC], F32))
        dxc_all = E(nc.sbuf_tensor("dxc_all", [128, IT * LC], BF16))
        y_all = E(nc.sbuf_tensor("y_all", [128, IT * LC], F32))
        yraw = E(nc.sbuf_tensor("yraw_sb", [128, IT * LC], BF16))
        brep = E(nc.sbuf_tensor("brep", [128, LC], BF16))
        crep = E(nc.sbuf_tensor("crep", [128, LC], BF16))
        dA_t = [E(nc.sbuf_tensor(f"dA_t{i}", [128, LC], F32)) for i in range(2)]
        dbu_t = E(nc.sbuf_tensor("dbu_t", [128, LC], BF16))
        h_t = [E(nc.sbuf_tensor(f"h_t{i}", [128, LC], BF16)) for i in range(2)]
        ym_t = [E(nc.sbuf_tensor(f"ym_t{i}", [128, LC], BF16)) for i in range(2)]
        sp_a = [E(nc.sbuf_tensor(f"sp_a{i}", [128, LC], F32)) for i in range(2)]
        sp_b = [E(nc.sbuf_tensor(f"sp_b{i}", [128, LC], F32)) for i in range(2)]

        def _wload(gq):
            g.sg.inc(gq.dma_start(out=dtw[:, :], in_=g.w_dtw), 16)
            g.sg.inc(gq.dma_start(out=oh16_sb[:, :], in_=g.c_oh16), 16)
        b.gpsimd(_wload)
        wl_n = g.sg.n

        chunks = [(bb, cc) for bb in range(BC) for cc in range(NCH)]
        xc_ld = []
        yraw_v = []
        dA_end = []
        pair_dA = []
        pair_v = []
        hcopy = []
        pacc = []
        bank_g = {i: Guard() for i in range(7)}

        for ci, (bb, cc) in enumerate(chunks):
            xcb = xc2[ci % 2]
            col = bb * L + cc * LC

            def _ld(sync, ci=ci, bb=bb, cc=cc):
                if ci >= 2:
                    sync.wait_ge(g.sv.h, yraw_v[ci - 2])
                if ci == 0:
                    sync.wait_ge(g.sg.h, wl_n)
                g.sy.inc(sync.dma_start(
                    out=xc2[ci % 2][:, :].rearrange("p (i t) -> p i t", t=LC),
                    in_=g.h_xc[bb][:, cc * LC:(cc + 1) * LC].rearrange("(i p) t -> p i t", p=128)),
                    16)
            b.sync(_ld)
            xc_ld.append(g.sy.n)

            dtm = []
            spd = []
            spv = []

            def _dt_t(t, ci=ci, col=col):
                for it_ in range(IT):
                    bank = it_ % 2
                    if it_ >= 2:
                        t.wait_ge(g.ss.h, spd[it_ - 2])
                    else:
                        bank_g[bank].wait(t)
                    mm = t.matmul(g.pb[bank][:, 0:LC], dtw[:, it_ * 128:(it_ + 1) * 128],
                                  g.dt_sb[:, col:col + LC], start=True, stop=True)
                    dtm.append(g.st.inc(mm))
                    _sp(it_)

            def _sp(it_, ci=ci):
                def _sp_s(s, it_=it_):
                    bank = it_ % 2
                    s.wait_ge(g.st.h, dtm[it_])
                    if it_ >= 2:
                        s.wait_ge(g.sv.h, spv[it_ - 2])
                    s.activation(out=sp_a[it_ % 2][:, :], in_=g.pb[bank][:, 0:LC], func=AF.Relu,
                                 bias=g.dtb[:, it_:it_ + 1])
                    s.activation(out=sp_b[it_ % 2][:, :], in_=g.pb[bank][:, 0:LC], func=AF.Abs,
                                 bias=g.dtb[:, it_:it_ + 1])
                    s.activation(out=sp_b[it_ % 2][:, :], in_=sp_b[it_ % 2][:, :], func=AF.Exp,
                                 scale=-1.0)
                    sc = s.activation(out=sp_b[it_ % 2][:, :], in_=sp_b[it_ % 2][:, :],
                                      func=AF.Ln, bias=g.c1[0:128, 0:1])
                    spd.append(g.ss.inc(sc))
                b.scalar(_sp_s)

                def _sp_v(v, it_=it_, ci=ci):
                    v.wait_ge(g.ss.h, spd[it_])
                    if it_ == 0:
                        v.wait_ge(g.sy.h, xc_ld[ci])
                        if ci > 0:
                            v.wait_ge(g.ss.h, dA_end[ci - 1])
                    v.tensor_tensor(out=delta_all[:, it_ * LC:(it_ + 1) * LC],
                                    in0=sp_a[it_ % 2][:, :], in1=sp_b[it_ % 2][:, :], op=OP.add)
                    dx = v.tensor_tensor(out=dxc_all[:, it_ * LC:(it_ + 1) * LC],
                                         in0=delta_all[:, it_ * LC:(it_ + 1) * LC],
                                         in1=xcb[:, it_ * LC:(it_ + 1) * LC], op=OP.mult)
                    spv.append(g.sv.inc(dx))
                b.vector(_sp_v)
            b.tensor(_dt_t)
            bank_g[0].set(g.ss, spd[-2])
            bank_g[1].set(g.ss, spd[-1])
            delta_v = spv[-1]

            def _pclear(p, ci=ci):
                if ci > 0:
                    p.wait_ge(g.sv.h, yraw_v[ci - 1])
                g.sp.inc(p.memset(y_all[:, :], 0.0))
            b.gpsimd(_pclear)
            pclear_n = g.sp.n

            for n in range(NS):
                bB = 2 + (n % 2)
                bC = 4 + (n % 2)

                def _rt(t, n=n, bB=bB, bC=bC, col=col):
                    bank_g[bB].wait(t)
                    bank_g[bC].wait(t)
                    t.matmul(g.pb[bB][:, 0:LC], oh16_sb[:, n * 128:(n + 1) * 128],
                             g.Brow[:, col:col + LC], start=True, stop=True)
                    mm = t.matmul(g.pb[bC][:, 0:LC], oh16_sb[:, n * 128:(n + 1) * 128],
                                  g.Crow[:, col:col + LC], start=True, stop=True)
                    rep_t.append(g.st.inc(mm))
                rep_t = []
                b.tensor(_rt)
                rt_n = g.st.n

                def _re(s, n=n, bB=bB, bC=bC):
                    s.wait_ge(g.st.h, rt_n)
                    if pair_v:
                        s.wait_ge(g.sv.h, pair_v[-1])
                    s.copy(out=brep[:, :], in_=g.pb[bB][:, 0:LC])
                    ev = s.copy(out=crep[:, :], in_=g.pb[bC][:, 0:LC])
                    rep_s.append(g.ss.inc(ev))
                rep_s = []
                b.scalar(_re)
                bank_g[bB].set(g.ss, rep_s[-1])
                bank_g[bC].set(g.ss, rep_s[-1])
                ren = rep_s[-1]

                for it_ in range(IT):
                    kg = len(pair_v)

                    def _dA2(s, n=n, it_=it_, kg=kg, delta_v=delta_v):
                        if kg >= 2:
                            s.wait_ge(g.sv.h, pair_v[kg - 2])
                        s.wait_ge(g.sv.h, delta_v)
                        a = s.activation(out=dA_t[kg % 2][:, :],
                                         in_=delta_all[:, it_ * LC:(it_ + 1) * LC], func=AF.Exp,
                                         scale=g.A_sb[:, it_ * NS + n:it_ * NS + n + 1])
                        pair_dA.append(g.ss.inc(a))
                    b.scalar(_dA2)

                    def _pv(v, n=n, it_=it_, kg=kg, ren=ren):
                        v.wait_ge(g.ss.h, pair_dA[kg])
                        v.wait_ge(g.ss.h, ren)
                        if kg >= 2:
                            v.wait_ge(g.ss.h, hcopy[kg - 2])
                            v.wait_ge(g.sp.h, pacc[kg - 2])
                        v.tensor_tensor(out=dbu_t[:, :],
                                        in0=dxc_all[:, it_ * LC:(it_ + 1) * LC],
                                        in1=brep[:, :], op=OP.mult)
                        v.tensor_tensor_scan(out=h_t[kg % 2][:, :], data0=dA_t[kg % 2][:, :],
                                             data1=dbu_t[:, :],
                                             initial=g.hstate[:, it_ * NS + n:it_ * NS + n + 1],
                                             op0=OP.mult, op1=OP.add)
                        ym = v.tensor_tensor(out=ym_t[kg % 2][:, :], in0=h_t[kg % 2][:, :],
                                             in1=crep[:, :], op=OP.mult)
                        pair_v.append(g.sv.inc(ym))
                    b.vector(_pv)

                    def _hc(s, n=n, it_=it_, kg=kg):
                        s.wait_ge(g.sv.h, pair_v[kg])
                        hc = s.copy(out=g.hstate[:, it_ * NS + n:it_ * NS + n + 1],
                                    in_=h_t[kg % 2][:, LC - 1:LC])
                        hcopy.append(g.ss.inc(hc))
                    b.scalar(_hc)

                    def _pa(p, it_=it_, kg=kg):
                        p.wait_ge(g.sv.h, pair_v[kg])
                        p.wait_ge(g.sp.h, pclear_n)
                        ac = p.tensor_tensor(out=y_all[:, it_ * LC:(it_ + 1) * LC],
                                             in0=y_all[:, it_ * LC:(it_ + 1) * LC],
                                             in1=ym_t[kg % 2][:, :], op=OP.add)
                        pacc.append(g.sp.inc(ac))
                    b.gpsimd(_pa)
            dA_end.append(pair_dA[-1])

            def _yraw(v, ci=ci):
                v.wait_ge(g.sp.h, pacc[-1])
                last = None
                for it_ in range(IT):
                    last = v.scalar_tensor_tensor(out=yraw[:, it_ * LC:(it_ + 1) * LC],
                                                  in0=xcb[:, it_ * LC:(it_ + 1) * LC],
                                                  scalar=g.Dv[:, it_:it_ + 1],
                                                  in1=y_all[:, it_ * LC:(it_ + 1) * LC],
                                                  op0=OP.mult, op1=OP.add)
                g.sv.inc(last)
            b.vector(_yraw)
            yraw_v.append(g.sv.n)
            yn = g.sv.n

            def _yst(sync, bb=bb, cc=cc):
                sync.wait_ge(g.sv.h, yn)
                g.sy.inc(sync.dma_start(
                    out=g.h_yraw[bb][:, cc * LC:(cc + 1) * LC].rearrange("(i p) t -> p i t", p=128),
                    in_=yraw[:, :].rearrange("p (i t) -> p i t", t=LC)), 16)
            b.sync(_yst)
            yraw_st = g.sy.n
            if ci + 1 < len(chunks):
                def _yg(v, yraw_st=yraw_st):
                    v.wait_ge(g.sy.h, yraw_st)
                b.vector(_yg)


def emit_p2(g):
    nc, b = g.nc, g.block
    QT = 1024
    NTC = QT // 512
    NHQ = 4
    with ExitStack() as c2:
        E = c2.enter_context
        acc = E(nc.sbuf_tensor("acc_sb", [128, JT * QT], F32))
        eg_sl = [E(nc.sbuf_tensor(f"eg_sl{i}", [128, JT * 512], BF16)) for i in range(2)]
        eu_sl = [E(nc.sbuf_tensor(f"eu_sl{i}", [128, JT * 512], BF16)) for i in range(2)]
        ed_sl = [E(nc.sbuf_tensor(f"ed_sl{i}", [128, 4 * DM], BF16)) for i in range(4)]
        xn_tc = [E(nc.sbuf_tensor(f"xn_tc{i}", [128, JT * 512], BF16)) for i in range(2)]
        xf_sb = E(nc.sbuf_tensor("xf_sb", [128, JT * 512], F32))
        rowseg = E(nc.sbuf_tensor("rowseg", [1, 2 * 512], F32))
        bc2 = E(nc.sbuf_tensor("bc2", [128, 2 * 512], F32))
        hid = E(nc.sbuf_tensor("hid", [128, NTC * IT * 512], BF16))
        sgate = E(nc.sbuf_tensor("sgate", [128, 512], BF16))
        cb_sb = E(nc.sbuf_tensor("cb_sb", [128, QT], BF16))
        comb_cm = E(nc.sbuf_tensor("comb_cm", [8, QT], BF16))
        lgq = E(nc.sbuf_tensor("lgq", [8, 512], F32))
        ex_sb = E(nc.sbuf_tensor("ex_sb", [128, 8], F32))
        eq1 = E(nc.sbuf_tensor("eq1", [128, 8], F32))
        eq2 = E(nc.sbuf_tensor("eq2", [128, 8], F32))
        msk = E(nc.sbuf_tensor("msk", [128, 8], F32))
        colv = [E(nc.sbuf_tensor(f"colv{i}", [128, 1], F32)) for i in range(6)]
        comb_tm = E(nc.sbuf_tensor("comb_tm", [128, 8], BF16))

        bank_g = {i: Guard() for i in range(7)}
        bankT_g = Guard()
        eg_guard = [Guard(), Guard()]
        ed_guard = [Guard() for _ in range(4)]
        hid_guard = Guard()
        acc_free = Guard()
        comb_war = Guard()
        cb_war = Guard()
        lgq_g = Guard()
        xn_guard = [Guard(), Guard()]
        gst_guard = Guard()
        ed_ld = {}
        ct_hist = []
        hid_done = {}
        units = [(ve, hq) for ve in range(NE + 1) for hq in range(NHQ)]

        for q in range(4):
            bb = q // 2
            tok0 = (q % 2) * QT

            cmb_ev = []
            for rc in range(NTC):
                colq = tok0 + rc * 512

                def _rld(sync, bb=bb, colq=colq):
                    sync.wait_ge(g.sv.h, g.sv.n)
                    g.sy.inc(sync.dma_start(
                        out=xf_sb[:, :].rearrange("p (j t) -> p j t", t=512),
                        in_=g.xt[bb][:, colq:colq + 512].rearrange("(j p) t -> p j t", p=128)), 16)
                    off = bb * L + colq
                    g.sy.inc(sync.dma_start(out=rowseg[:, 0:512],
                                            in_=g.h_rows[0:1, off:off + 512]), 16)
                    g.sy.inc(sync.dma_start(out=rowseg[:, 512:1024],
                                            in_=g.h_rows[1:2, off:off + 512]), 16)
                b.sync(_rld)
                rld = g.sy.n

                bcm = []
                bce = []
                for qq in range(2):
                    def _bt(t, qq=qq):
                        if qq == 0:
                            t.wait_ge(g.sy.h, rld)
                        bank_g[0].wait(t)
                        mm = t.matmul(g.pb[0][:, 0:512], g.ones1_f[:, :],
                                      rowseg[:, qq * 512:(qq + 1) * 512], start=True, stop=True)
                        bcm.append(g.st.inc(mm))
                    b.tensor(_bt)

                    def _bs(s, qq=qq):
                        s.wait_ge(g.st.h, bcm[-1])
                        ev = s.copy(out=bc2[:, qq * 512:(qq + 1) * 512], in_=g.pb[0][:, 0:512])
                        bce.append(g.ss.inc(ev))
                    b.scalar(_bs)
                    bank_g[0].set(g.ss, bce[-1])
                bcn = bce[-1]

                def _rxn(v):
                    v.wait_ge(g.ss.h, bcn)
                    v.wait_ge(g.sy.h, rld)
                    last = None
                    for j in range(JT):
                        sl = xf_sb[:, j * 512:(j + 1) * 512]
                        v.tensor_tensor(out=sl, in0=sl, in1=bc2[:, 0:512], op=OP.mult)
                        last = v.tensor_tensor(out=sl, in0=sl, in1=bc2[:, 512:1024],
                                               op=OP.subtract)
                    g.sv.inc(last)
                b.vector(_rxn)
                xnf_n = g.sv.n

                lgm = []

                def _lg(t):
                    t.wait_ge(g.sv.h, xnf_n)
                    bank_g[1].wait(t)
                    mm = None
                    for k in range(JT):
                        mm = t.matmul(g.pb[1][0:8, 0:512], g.rt_sb[:, k * NE:(k + 1) * NE],
                                      xf_sb[:, k * 512:(k + 1) * 512],
                                      start=(k == 0), stop=(k == JT - 1))
                    lgm.append(g.st.inc(mm))
                b.tensor(_lg)

                def _lge(s):
                    s.wait_ge(g.st.h, lgm[-1])
                    lgq_g.wait(s)
                    g.ss.inc(s.copy(out=lgq[:, :], in_=g.pb[1][0:8, 0:512]))
                b.scalar(_lge)
                lgen = g.ss.n
                bank_g[1].set(g.ss, lgen)

                for tt in range(4):
                    tpm = []

                    def _tp(t, tt=tt):
                        t.wait_ge(g.ss.h, lgen)
                        bank_g[2].wait(t)
                        mm = t.transpose(g.pb[2][0:128, 0:8], lgq[0:8, tt * 128:(tt + 1) * 128],
                                         g.idf_sb[0:8, 0:8])
                        tpm.append(g.st.inc(mm))
                    b.tensor(_tp)
                    prev_ctn = ct_hist[-1] if ct_hist else 0
                    tpn = g.st.n
                    lgq_g.add(g.st, tpn)

                    def _sm1(v):
                        v.wait_ge(g.st.h, tpn)
                        v.tensor_reduce(out=colv[0][:, :], in_=g.pb[2][0:128, 0:8], axis=AX,
                                        op=OP.max)
                        v.drain()
                        g.sv.inc(v.tensor_scalar(out=colv[1][:, :], in0=colv[0][:, :],
                                                 scalar1=-1.0, scalar2=None, op0=OP.mult))
                    b.vector(_sm1)
                    smn = g.sv.n

                    def _sm2(s):
                        s.wait_ge(g.sv.h, smn)
                        g.ss.inc(s.activation(out=ex_sb[:, :], in_=g.pb[2][0:128, 0:8],
                                              func=AF.Exp, bias=colv[1][:, 0:1]))
                    b.scalar(_sm2)
                    exn = g.ss.n
                    bank_g[2].set(g.ss, exn)

                    def _top2(v, prev_ctn=prev_ctn):
                        v.wait_ge(g.ss.h, exn)
                        if prev_ctn:
                            v.wait_ge(g.st.h, prev_ctn)
                        v.tensor_reduce(out=colv[2][:, :], in_=ex_sb[:, :], axis=AX, op=OP.max)
                        v.drain()
                        v.tensor_scalar(out=eq1[:, :], in0=ex_sb[:, :], scalar1=colv[2][:, 0:1],
                                        scalar2=None, op0=OP.is_equal)
                        v.drain()
                        v.tensor_scalar(out=msk[:, :], in0=eq1[:, :], scalar1=BIG, scalar2=None,
                                        op0=OP.mult)
                        v.drain()
                        v.tensor_tensor(out=msk[:, :], in0=ex_sb[:, :], in1=msk[:, :],
                                        op=OP.subtract)
                        v.drain()
                        v.tensor_reduce(out=colv[3][:, :], in_=msk[:, :], axis=AX, op=OP.max)
                        v.drain()
                        v.tensor_tensor(out=colv[4][:, :], in0=colv[2][:, :], in1=colv[3][:, :],
                                        op=OP.add)
                        v.drain()
                        v.reciprocal(out=colv[5][:, :], in_=colv[4][:, :])
                        v.drain()
                        v.tensor_tensor(out=colv[4][:, :], in0=colv[2][:, :], in1=colv[5][:, :],
                                        op=OP.mult)
                        v.tensor_tensor(out=colv[5][:, :], in0=colv[3][:, :], in1=colv[5][:, :],
                                        op=OP.mult)
                        v.drain()
                        v.tensor_scalar(out=eq2[:, :], in0=msk[:, :], scalar1=colv[3][:, 0:1],
                                        scalar2=None, op0=OP.is_equal)
                        v.drain()
                        v.tensor_scalar(out=eq2[:, :], in0=eq2[:, :], scalar1=colv[5][:, 0:1],
                                        scalar2=None, op0=OP.mult)
                        v.drain()
                        tm = v.scalar_tensor_tensor(out=comb_tm[:, :], in0=eq1[:, :],
                                                    scalar=colv[4][:, 0:1], in1=eq2[:, :],
                                                    op0=OP.mult, op1=OP.add)
                        g.sv.inc(tm)
                    b.vector(_top2)
                    t2n = g.sv.n

                    ctm = []

                    def _ct(t):
                        t.wait_ge(g.sv.h, t2n)
                        bankT_g.wait(t)
                        mm = t.transpose(g.pbT[0:8, 0:128], comb_tm[:, 0:8], g.id_sb[:, :])
                        ctm.append(g.st.inc(mm))
                    b.tensor(_ct)
                    ctn = g.st.n
                    ct_hist.append(ctn)

                    def _ce(s, tt=tt, rc=rc):
                        s.wait_ge(g.st.h, ctn)
                        comb_war.wait(s)
                        ev = s.copy(out=comb_cm[0:8, rc * 512 + tt * 128:rc * 512 + (tt + 1) * 128],
                                    in_=g.pbT[0:8, 0:128])
                        cmb_ev.append(g.ss.inc(ev))
                    b.scalar(_ce)
                    bankT_g.set(g.ss, cmb_ev[-1])
            comb_n = cmb_ev[-1]

            def _comb_dump(sync, q=q):
                sync.wait_ge(g.ss.h, comb_n)
                g.sy.inc(sync.dma_start(out=g.h_comb[q], in_=comb_cm[0:8, :]), 16)
            b.sync(_comb_dump)

            shared_acc_ref = [0]
            cur_cbn = 0
            for ui, (ve, hq) in enumerate(units):
                gui = q * len(units) + ui
                if ve == 0:
                    sg_, su_, sd_ = g.w_sg, g.w_su, g.w_sd
                else:
                    sg_, su_, sd_ = g.w_eg[ve - 1], g.w_eu[ve - 1], g.w_ed[ve - 1]
                hc0 = hq * 512
                egb = eg_sl[gui % 2]
                eub = eu_sl[gui % 2]
                edb = ed_sl[hq]

                def _slab(gq, sg_=sg_, su_=su_, sd_=sd_, hc0=hc0, egb=egb, eub=eub, edb=edb,
                          gui=gui, hq=hq):
                    eg_guard[gui % 2].wait(gq)
                    g.sg.inc(gq.dma_start(
                        out=egb[:, :].rearrange("p (k m) -> p k m", m=512),
                        in_=sg_[:, hc0:hc0 + 512].rearrange("(k p) m -> p k m", p=128)), 16)
                    g.sg.inc(gq.dma_start(
                        out=eub[:, :].rearrange("p (k m) -> p k m", m=512),
                        in_=su_[:, hc0:hc0 + 512].rearrange("(k p) m -> p k m", p=128)), 16)
                    ed_guard[hq].wait(gq)
                    g.sg.inc(gq.dma_start(
                        out=edb[:, :].rearrange("p (k m) -> p k m", m=DM),
                        in_=sd_[hc0:hc0 + 512, :].rearrange("(k p) m -> p k m", p=128)), 16)
                b.gpsimd(_slab)
                slab_n = g.sg.n
                ed_ld[(ve, hq)] = slab_n

                if ve >= 1 and hq == 0:
                    cbm = []
                    cbe = []
                    for tc in range(NTC):
                        def _cb(t, ve=ve, tc=tc):
                            t.wait_ge(g.ss.h, comb_n)
                            bank_g[0].wait(t)
                            mm = t.matmul(g.pb[0][:, 0:512],
                                          g.oh8_sb[:, (ve - 1) * 128:ve * 128],
                                          comb_cm[0:8, tc * 512:(tc + 1) * 512],
                                          start=True, stop=True)
                            cbm.append(g.st.inc(mm))
                        b.tensor(_cb)
                        comb_war.add(g.st, g.st.n)

                        def _cbe(s, tc=tc):
                            s.wait_ge(g.st.h, cbm[-1])
                            if tc == 0:
                                cb_war.wait(s)
                            ev = s.copy(out=cb_sb[:, tc * 512:(tc + 1) * 512],
                                        in_=g.pb[0][:, 0:512])
                            cbe.append(g.ss.inc(ev))
                        b.scalar(_cbe)
                        bank_g[0].set(g.ss, cbe[-1])
                    cur_cbn = cbe[-1]
                cbn = cur_cbn if ve >= 1 else 0

                for tc in range(NTC):
                    def _xnld(sync, tc=tc, bb=bb, tok0=tok0):
                        xn_guard[tc % 2].wait(sync)
                        colq = tok0 + tc * 512
                        g.sy.inc(sync.dma_start(
                            out=xn_tc[tc % 2][:, :].rearrange("p (j t) -> p j t", t=512),
                            in_=g.h_xn[bb][:, colq:colq + 512].rearrange(
                                "(j p) t -> p j t", p=128)), 16)
                    b.sync(_xnld)
                    xnld_n = g.sy.n

                    gm = []
                    um = []
                    ge = []
                    hv = []

                    def _hid_one(m, tc=tc, hq=hq, ve=ve, cbn=cbn):
                        bg_ = 2 + (m % 2)
                        bu_ = 4 + (m % 2)

                        def _sev(s, m=m):
                            s.wait_ge(g.st.h, gm[m])
                            ev = s.activation(out=sgate[:, :], in_=g.pb[bg_][:, 0:512],
                                              func=AF.Silu)
                            ge.append(g.ss.inc(ev))
                        b.scalar(_sev)
                        sevn = g.ss.n

                        def _hvf(v, m=m):
                            v.wait_ge(g.ss.h, sevn)
                            v.wait_ge(g.st.h, um[m])
                            if m == 0:
                                hid_guard.wait(v)
                            mh = (tc * IT) + hq * 4 + m
                            sl = hid[:, mh * 512:(mh + 1) * 512]
                            t1 = v.tensor_tensor(out=sl, in0=sgate[:, :],
                                                 in1=g.pb[bu_][:, 0:512], op=OP.mult)
                            if ve >= 1:
                                if cbn:
                                    v.wait_ge(g.ss.h, cbn)
                                t1 = v.tensor_tensor(out=sl, in0=sl,
                                                     in1=cb_sb[:, tc * 512:(tc + 1) * 512],
                                                     op=OP.mult)
                            hv.append(g.sv.inc(t1))
                        b.vector(_hvf)

                    def _gu(t, tc=tc, egb=egb, eub=eub, slab_n=slab_n, xnld_n=xnld_n):
                        t.wait_ge(g.sg.h, slab_n)
                        t.wait_ge(g.sy.h, xnld_n)
                        xb = xn_tc[tc % 2]
                        for m in range(4):
                            bg_ = 2 + (m % 2)
                            bu_ = 4 + (m % 2)
                            if m >= 2:
                                t.wait_ge(g.ss.h, ge[m - 2])
                                t.wait_ge(g.sv.h, hv[m - 2])
                            else:
                                bank_g[bg_].wait(t)
                                bank_g[bu_].wait(t)
                            mm = None
                            for k in range(JT):
                                mm = t.matmul(g.pb[bg_][:, 0:512],
                                              egb[:, k * 512 + m * 128:k * 512 + (m + 1) * 128],
                                              xb[:, k * 512:(k + 1) * 512],
                                              start=(k == 0), stop=(k == JT - 1))
                            gm.append(g.st.inc(mm))
                            for k in range(JT):
                                mm = t.matmul(g.pb[bu_][:, 0:512],
                                              eub[:, k * 512 + m * 128:k * 512 + (m + 1) * 128],
                                              xb[:, k * 512:(k + 1) * 512],
                                              start=(k == 0), stop=(k == JT - 1))
                            um.append(g.st.inc(mm))
                            _hid_one(m)
                    b.tensor(_gu)
                    bank_g[2].set(g.ss, ge[-2])
                    bank_g[3].set(g.ss, ge[-1])
                    bank_g[4].set(g.sv, hv[-2])
                    bank_g[5].set(g.sv, hv[-1])
                    xn_guard[tc % 2].set(g.st, g.st.n)
                    if hq == NHQ - 1:
                        hid_done[tc] = hv[-1]

                        dnm = []

                        def _dn(t, tc=tc, ve=ve):
                            for hh in range(NHQ):
                                t.wait_ge(g.sg.h, ed_ld[(ve, hh)])
                            t.wait_ge(g.sv.h, hid_done[tc])
                            for mo in range(JT):
                                bank_g[6].wait(t)
                                mm = None
                                for hh in range(NHQ):
                                    for kl in range(4):
                                        mh = (tc * IT) + hh * 4 + kl
                                        mm = t.matmul(
                                            g.pb[6][:, 0:512],
                                            ed_sl[hh][:, kl * DM + mo * 128:
                                                      kl * DM + (mo + 1) * 128],
                                            hid[:, mh * 512:(mh + 1) * 512],
                                            start=(hh == 0 and kl == 0),
                                            stop=(hh == NHQ - 1 and kl == 3))
                                dnm.append(g.st.inc(mm))
                                _acc_one(mo)

                        def _acc_one(mo, tc=tc, ve=ve):
                            sl_lo = mo * QT + tc * 512

                            def _as(s, mo=mo):
                                s.wait_ge(g.st.h, dnm[mo])
                                acc_free.wait(s)
                                ac = s.copy(out=acc[:, sl_lo:sl_lo + 512],
                                            in_=g.pb[6][:, 0:512])
                                bank_g[6].set(g.ss, g.ss.inc(ac))

                            def _av(v, mo=mo):
                                v.wait_ge(g.st.h, dnm[mo])
                                v.wait_ge(g.ss.h, shared_acc_ref[0])
                                ac = v.tensor_tensor(out=acc[:, sl_lo:sl_lo + 512],
                                                     in0=acc[:, sl_lo:sl_lo + 512],
                                                     in1=g.pb[6][:, 0:512], op=OP.add)
                                bank_g[6].set(g.sv, g.sv.inc(ac))
                            if ve == 0:
                                b.scalar(_as)
                            else:
                                b.vector(_av)
                        b.tensor(_dn)
                        if ve == 0 and tc == NTC - 1:
                            shared_acc_ref[0] = g.ss.n
                eg_guard[gui % 2].set(g.st, g.st.n)
                if hq == NHQ - 1:
                    dn_end = g.st.n
                    for hh in range(NHQ):
                        ed_guard[hh].set(g.st, dn_end)
                    hid_guard.set(g.st, dn_end)
                    cb_war.set(g.sv, g.sv.n)

            for tc in range(NTC):
                gss = []

                gbuf = xn_tc[tc % 2]

                def _gs(s, tc=tc, gbuf=gbuf):
                    s.wait_ge(g.sv.h, g.sv.n)
                    gst_guard.wait(s)
                    last = None
                    for mo in range(JT):
                        last = s.activation(
                            out=gbuf[:, mo * 512:(mo + 1) * 512],
                            in_=acc[:, mo * QT + tc * 512:mo * QT + (tc + 1) * 512],
                            func=AF.Silu)
                    gss.append(g.ss.inc(last))
                b.scalar(_gs)

                def _gst(sync, tc=tc, bb=bb, tok0=tok0, gbuf=gbuf):
                    sync.wait_ge(g.ss.h, gss[-1])
                    colq = tok0 + tc * 512
                    g.sy.inc(sync.dma_start(
                        out=g.h_g[bb][:, colq:colq + 512].rearrange("(j p) t -> p j t", p=128),
                        in_=gbuf[:, :].rearrange("p (j t) -> p j t", t=512)), 16)
                b.sync(_gst)
                gst_guard.set(g.sy, g.sy.n)
                xn_guard[tc % 2].add(g.sy, g.sy.n)
            acc_free.set(g.ss, g.ss.n)


def emit_p1c(g):
    nc, b = g.nc, g.block
    with ExitStack() as c2:
        E = c2.enter_context
        outw = E(nc.sbuf_tensor("outw_sl", [128, IT * DM], BF16))
        ffg = E(nc.sbuf_tensor("ffg_sl", [128, JT * 2 * DM], BF16))
        ffp = E(nc.sbuf_tensor("ffp_sl", [128, JT * DM], BF16))
        yraw2 = E(nc.sbuf_tensor("yraw2", [128, IT * LC], BF16))
        sz2 = E(nc.sbuf_tensor("sz2", [128, IT * LC], BF16))
        xf2 = E(nc.sbuf_tensor("xf2", [128, JT * LC], F32))
        sq2 = [E(nc.sbuf_tensor(f"sq2_{i}", [128, LC], F32)) for i in range(2)]
        tmp2 = E(nc.sbuf_tensor("tmp2", [128, LC], F32))
        ln2 = E(nc.sbuf_tensor("ln2_sb", [128, JT * LC], BF16))
        sg2 = E(nc.sbuf_tensor("sg2", [128, JT * LC], BF16))
        ff2 = E(nc.sbuf_tensor("ff2", [128, JT * LC], BF16))
        rowseg = E(nc.sbuf_tensor("rowseg2", [1, 2 * LC], F32))
        bc3 = E(nc.sbuf_tensor("bc3", [128, 4 * LC], F32))
        rtmp = [E(nc.sbuf_tensor(f"rtmpc{i}", [1, LC], F32)) for i in range(2)]

        def _wload(gq):
            g.sg.inc(gq.dma_start(
                out=outw[:, :].rearrange("p (k m) -> p k m", m=DM),
                in_=g.w_outw.rearrange("(k p) m -> p k m", p=128)), 16)
            g.sg.inc(gq.dma_start(
                out=ffg[:, :].rearrange("p (k m) -> p k m", m=2 * DM),
                in_=g.w_ffg.rearrange("(k p) m -> p k m", p=128)), 16)
            g.sg.inc(gq.dma_start(
                out=ffp[:, :].rearrange("p (k m) -> p k m", m=DM),
                in_=g.w_ffp.rearrange("(k p) m -> p k m", p=128)), 16)
        b.gpsimd(_wload)
        wl_n = g.sg.n

        chunks = [(bb, cc) for bb in range(BC) for cc in range(NCH)]
        lds = []
        sst = []
        chunk_done = []
        bank_g = {i: Guard() for i in range(7)}
        act_g = Guard()

        for ci, (bb, cc) in enumerate(chunks):
            yb, zb, xb = yraw2, sz2, xf2

            def _ld(sync, ci=ci, bb=bb, cc=cc):
                if ci >= 1:
                    sync.wait_ge(g.sv.h, chunk_done[ci - 1])
                    sync.wait_ge(g.sy.h, sst[ci - 1])
                col = cc * LC
                g.sy.inc(sync.dma_start(
                    out=yraw2[:, :].rearrange("p (i t) -> p i t", t=LC),
                    in_=g.h_yraw[bb][:, col:col + LC].rearrange("(i p) t -> p i t", p=128)), 16)
                g.sy.inc(sync.dma_start(
                    out=sz2[:, :].rearrange("p (i t) -> p i t", t=LC),
                    in_=g.h_sz[bb][:, col:col + LC].rearrange("(i p) t -> p i t", p=128)), 16)
                g.sy.inc(sync.dma_start(
                    out=xf2[:, :].rearrange("p (j t) -> p j t", t=LC),
                    in_=g.xt[bb][:, col:col + LC].rearrange("(j p) t -> p j t", p=128)), 16)
                off = bb * L + col
                g.sy.inc(sync.dma_start(out=rowseg[:, 0:LC],
                                        in_=g.h_rows[0:1, off:off + LC]), 16)
                g.sy.inc(sync.dma_start(out=rowseg[:, LC:2 * LC],
                                        in_=g.h_rows[1:2, off:off + LC]), 16)
            b.sync(_ld)
            lds.append(g.sy.n)

            def _yf(v, ci=ci):
                v.wait_ge(g.sy.h, lds[ci])
                act_g.wait(v)
                last = None
                for it_ in range(IT):
                    last = v.tensor_tensor(out=yb[:, it_ * LC:(it_ + 1) * LC],
                                           in0=yb[:, it_ * LC:(it_ + 1) * LC],
                                           in1=zb[:, it_ * LC:(it_ + 1) * LC], op=OP.mult)
                g.sv.inc(last)
            b.vector(_yf)
            yf_n = g.sv.n

            bcm = []
            bce = []
            for qq in range(2):
                def _bt(t, qq=qq, ci=ci):
                    if qq == 0:
                        t.wait_ge(g.sy.h, lds[ci])
                    bank_g[6].wait(t)
                    mm = t.matmul(g.pb[6][:, 0:LC], g.ones1_f[:, :],
                                  rowseg[:, qq * LC:(qq + 1) * LC], start=True, stop=True)
                    bcm.append(g.st.inc(mm))
                b.tensor(_bt)

                def _bs(s, qq=qq):
                    s.wait_ge(g.st.h, bcm[-1])
                    ev = s.copy(out=bc3[:, qq * LC:(qq + 1) * LC], in_=g.pb[6][:, 0:LC])
                    bce.append(g.ss.inc(ev))
                b.scalar(_bs)
                bank_g[6].set(g.ss, bce[-1])
            bcn = bce[-1]

            opm = []
            x2v = []

            def _op(t, ci=ci):
                t.wait_ge(g.sv.h, yf_n)
                if ci == 0:
                    t.wait_ge(g.sg.h, wl_n)
                for mo in range(JT):
                    bank = mo % 2
                    if mo >= 2:
                        t.wait_ge(g.sv.h, x2v[mo - 2])
                    else:
                        bank_g[bank].wait(t)
                    mm = None
                    for k in range(IT):
                        mm = t.matmul(g.pb[bank][:, 0:LC],
                                      outw[:, k * DM + mo * 128:k * DM + (mo + 1) * 128],
                                      yb[:, k * LC:(k + 1) * LC],
                                      start=(k == 0), stop=(k == IT - 1))
                    opm.append(g.st.inc(mm))
                    _x2_one(mo)

            def _x2_one(mo):
                def _xv(v, mo=mo):
                    bank = mo % 2
                    v.wait_ge(g.st.h, opm[mo])
                    v.wait_ge(g.ss.h, bcn)
                    sl = xb[:, mo * LC:(mo + 1) * LC]
                    v.tensor_tensor(out=sl, in0=sl, in1=bc3[:, 0:LC], op=OP.mult)
                    v.tensor_tensor(out=sl, in0=sl, in1=bc3[:, LC:2 * LC], op=OP.subtract)
                    xv = v.tensor_tensor(out=sl, in0=sl, in1=g.pb[bank][:, 0:LC], op=OP.add)
                    x2v.append(g.sv.inc(xv))
                b.vector(_xv)
            b.tensor(_op)
            bank_g[0].set(g.sv, x2v[-2])
            bank_g[1].set(g.sv, x2v[-1])
            x2n = x2v[-1]

            sqv = []
            sqc = []
            for j in range(JT):
                def _sqv(v, j=j):
                    v.wait_ge(g.sv.h, x2v[j])
                    if j >= 2:
                        v.wait_ge(g.st.h, sqc[j - 2])
                    sq = v.tensor_tensor(out=sq2[j % 2][:, :], in0=xb[:, j * LC:(j + 1) * LC],
                                         in1=xb[:, j * LC:(j + 1) * LC], op=OP.mult)
                    sqv.append(g.sv.inc(sq))
                b.vector(_sqv)

                def _st2(t, j=j):
                    if j == 0:
                        bank_g[2].wait(t)
                        bank_g[3].wait(t)
                    t.wait_ge(g.sv.h, x2v[j])
                    t.matmul(g.pb[2][0:1, 0:LC], g.ones_f[:, :], xb[:, j * LC:(j + 1) * LC],
                             start=(j == 0), stop=(j == JT - 1))
                    t.wait_ge(g.sv.h, sqv[j])
                    mm = t.matmul(g.pb[3][0:1, 0:LC], g.ones_f[:, :], sq2[j % 2][:, :],
                                  start=(j == 0), stop=(j == JT - 1))
                    sqc.append(g.st.inc(mm))
                b.tensor(_st2)
            stn = g.st.n

            mu2, msq2 = rowseg[0:1, 0:LC], rowseg[0:1, LC:2 * LC]
            var2, vpe2 = rtmp[0][:, :], rtmp[1][:, :]

            def _r2A(v):
                v.wait_ge(g.st.h, stn)
                v.tensor_scalar(out=mu2, in0=g.pb[2][0:1, 0:LC], scalar1=1.0 / DM,
                                scalar2=None, op0=OP.mult)
                v.tensor_scalar(out=msq2, in0=g.pb[3][0:1, 0:LC], scalar1=1.0 / DM,
                                scalar2=None, op0=OP.mult)
                v.tensor_tensor(out=var2, in0=mu2, in1=mu2, op=OP.mult)
                g.sv.inc(v.tensor_tensor(out=var2, in0=msq2, in1=var2,
                                         op=OP.subtract))
            b.vector(_r2A)
            r2a = g.sv.n
            bank_g[2].set(g.sv, r2a)
            bank_g[3].set(g.sv, r2a)

            def _r2B(s):
                s.wait_ge(g.sv.h, r2a)
                s.activation(out=vpe2, in_=var2, func=AF.Ln, bias=g.cEPS[0:1, 0:1])
                g.ss.inc(s.activation(out=vpe2, in_=vpe2, func=AF.Exp, scale=-0.5))
            b.scalar(_r2B)
            r2b = g.ss.n

            def _r2C(v):
                v.wait_ge(g.ss.h, r2b)
                g.sv.inc(v.tensor_tensor(out=mu2, in0=mu2, in1=vpe2,
                                         op=OP.mult))
            b.vector(_r2C)
            r2c = g.sv.n

            b2m = []
            b2e = []
            for qq, src in [(0, vpe2), (1, mu2)]:
                def _b2t(t, qq=qq, src=src):
                    t.wait_ge(g.sv.h, r2c)
                    bank_g[6].wait(t)
                    mm = t.matmul(g.pb[6][:, 0:LC], g.ones1_f[:, :], src[:, :],
                                  start=True, stop=True)
                    b2m.append(g.st.inc(mm))
                b.tensor(_b2t)

                def _b2s(s, qq=qq):
                    s.wait_ge(g.st.h, b2m[-1])
                    ev = s.copy(out=bc3[:, (2 + qq) * LC:(3 + qq) * LC], in_=g.pb[6][:, 0:LC])
                    b2e.append(g.ss.inc(ev))
                b.scalar(_b2s)
                bank_g[6].set(g.ss, b2e[-1])
            b2n = b2e[-1]

            def _ln2(v):
                v.wait_ge(g.ss.h, b2n)
                last = None
                for j in range(JT):
                    v.tensor_tensor(out=tmp2[:, :], in0=xb[:, j * LC:(j + 1) * LC],
                                    in1=bc3[:, 2 * LC:3 * LC], op=OP.mult)
                    v.tensor_tensor(out=tmp2[:, :], in0=tmp2[:, :], in1=bc3[:, 3 * LC:4 * LC],
                                    op=OP.subtract)
                    last = v.tensor_scalar(out=ln2[:, j * LC:(j + 1) * LC], in0=tmp2[:, :],
                                           scalar1=g.n2g[:, j:j + 1], scalar2=g.n2b[:, j:j + 1],
                                           op0=OP.mult, op1=OP.add)
                g.sv.inc(last)
            b.vector(_ln2)
            ln2n = g.sv.n

            fgm = []
            fge = []

            def _ffg(t, ci=ci):
                t.wait_ge(g.sv.h, ln2n)
                for m in range(2 * JT):
                    bank = 2 + (m % 2)
                    if m >= 2:
                        eng, cnt = fge[m - 2]
                        t.wait_ge(g.ss.h if eng == "s" else g.sv.h, cnt)
                    else:
                        bank_g[bank].wait(t)
                    mm = None
                    for k in range(JT):
                        mm = t.matmul(g.pb[bank][:, 0:LC],
                                      ffg[:, k * 2 * DM + m * 128:k * 2 * DM + (m + 1) * 128],
                                      ln2[:, k * LC:(k + 1) * LC],
                                      start=(k == 0), stop=(k == JT - 1))
                    fgm.append(g.st.inc(mm))
                    _ffg_ev(m)

            def _ffg_ev(m):
                bank = 2 + (m % 2)
                if m < JT:
                    def _ev(s, m=m):
                        s.wait_ge(g.st.h, fgm[m])
                        ev = s.activation(out=sg2[:, m * LC:(m + 1) * LC],
                                          in_=g.pb[bank][:, 0:LC], func=AF.Silu,
                                          bias=g.ffgb[:, m:m + 1])
                        fge.append(("s", g.ss.inc(ev)))
                    b.scalar(_ev)
                else:
                    def _ev(v, m=m):
                        mj = m - JT
                        v.wait_ge(g.st.h, fgm[m])
                        eng, cnt = fge[mj]
                        v.wait_ge(g.ss.h, cnt)
                        ev = v.scalar_tensor_tensor(out=ff2[:, mj * LC:(mj + 1) * LC],
                                                    in0=g.pb[bank][:, 0:LC],
                                                    scalar=g.ffgb[:, JT + mj:JT + mj + 1],
                                                    in1=sg2[:, mj * LC:(mj + 1) * LC],
                                                    op0=OP.add, op1=OP.mult)
                        fge.append(("v", g.sv.inc(ev)))
                    b.vector(_ev)
            b.tensor(_ffg)
            e0, c0 = fge[-2]
            e1, c1 = fge[-1]
            bank_g[2].set(g.ss if e0 == "s" else g.sv, c0)
            bank_g[3].set(g.ss if e1 == "s" else g.sv, c1)
            ffn = c1

            fpm = []
            sv_ = []

            def _ffp(t, ci=ci):
                t.wait_ge(g.sv.h, ffn)
                for mo in range(JT):
                    bank = 4 + (mo % 2)
                    if mo >= 2:
                        t.wait_ge(g.sv.h, sv_[mo - 2])
                    else:
                        bank_g[bank].wait(t)
                    mm = None
                    for k in range(JT):
                        mm = t.matmul(g.pb[bank][:, 0:LC],
                                      ffp[:, k * DM + mo * 128:k * DM + (mo + 1) * 128],
                                      ff2[:, k * LC:(k + 1) * LC],
                                      start=(k == 0), stop=(k == JT - 1))
                    fpm.append(g.st.inc(mm))
                    _s_one(mo)

            def _s_one(mo):
                def _sv(v, mo=mo):
                    bank = 4 + (mo % 2)
                    v.wait_ge(g.st.h, fpm[mo])
                    sv2 = v.scalar_tensor_tensor(out=xb[:, mo * LC:(mo + 1) * LC],
                                                 in0=g.pb[bank][:, 0:LC],
                                                 scalar=g.ffpb[:, mo:mo + 1],
                                                 in1=xb[:, mo * LC:(mo + 1) * LC],
                                                 op0=OP.add, op1=OP.add)
                    sv_.append(g.sv.inc(sv2))
                b.vector(_sv)
            b.tensor(_ffp)
            bank_g[4].set(g.sv, sv_[-2])
            bank_g[5].set(g.sv, sv_[-1])
            sn = sv_[-1]

            def _sst(sync, bb=bb, cc=cc):
                sync.wait_ge(g.sv.h, sn)
                g.sy.inc(sync.dma_start(
                    out=g.h_s[bb][:, cc * LC:(cc + 1) * LC].rearrange("(j p) t -> p j t", p=128),
                    in_=xf2[:, :].rearrange("p (j t) -> p j t", t=LC)), 16)
            b.sync(_sst)
            sst.append(g.sy.n)

            chunk_done.append(g.sv.n)
            act_g.add(g.st, g.st.n)
            act_g.add(g.sy, sst[-1])
            act_g.add(g.ss, g.ss.n)


def emit_p3(g):
    nc, b = g.nc, g.block
    with ExitStack() as c2:
        E = c2.enter_context
        xf3 = [E(nc.sbuf_tensor(f"xf3_{i}", [128, JT * LC], F32)) for i in range(2)]
        sf3 = [E(nc.sbuf_tensor(f"sf3_{i}", [128, JT * LC], F32)) for i in range(2)]
        gf3 = [E(nc.sbuf_tensor(f"gf3_{i}", [128, JT * LC], BF16)) for i in range(2)]
        o3 = [E(nc.sbuf_tensor(f"o3_{i}", [128, JT * LC], F32)) for i in range(2)]

        chunks = [(bb, cc) for bb in range(BC) for cc in range(NCH)]
        lds = []
        ov = []
        st3 = []

        for ci, (bb, cc) in enumerate(chunks):
            def _ld(sync, ci=ci, bb=bb, cc=cc):
                if ci >= 2:
                    sync.wait_ge(g.sv.h, ov[ci - 2])
                col = cc * LC
                g.sy.inc(sync.dma_start(
                    out=xf3[ci % 2][:, :].rearrange("p (j t) -> p j t", t=LC),
                    in_=g.xt[bb][:, col:col + LC].rearrange("(j p) t -> p j t", p=128)), 16)
                g.sy.inc(sync.dma_start(
                    out=sf3[ci % 2][:, :].rearrange("p (j t) -> p j t", t=LC),
                    in_=g.h_s[bb][:, col:col + LC].rearrange("(j p) t -> p j t", p=128)), 16)
                g.sy.inc(sync.dma_start(
                    out=gf3[ci % 2][:, :].rearrange("p (j t) -> p j t", t=LC),
                    in_=g.h_g[bb][:, col:col + LC].rearrange("(j p) t -> p j t", p=128)), 16)
            b.sync(_ld)
            lds.append(g.sy.n)

            def _ov(v, ci=ci):
                v.wait_ge(g.sy.h, lds[ci])
                if ci >= 2:
                    v.wait_ge(g.sy.h, st3[ci - 2])
                last = None
                for j in range(JT):
                    sl = o3[ci % 2][:, j * LC:(j + 1) * LC]
                    v.tensor_tensor(out=sl, in0=sf3[ci % 2][:, j * LC:(j + 1) * LC],
                                    in1=gf3[ci % 2][:, j * LC:(j + 1) * LC], op=OP.mult)
                    last = v.tensor_tensor(out=sl, in0=sl,
                                           in1=xf3[ci % 2][:, j * LC:(j + 1) * LC], op=OP.add)
                g.sv.inc(last)
            b.vector(_ov)
            ov.append(g.sv.n)
            ovn = g.sv.n

            def _st(sync, ci=ci, bb=bb, cc=cc):
                sync.wait_ge(g.sv.h, ovn)
                col = cc * LC
                g.sy.inc(sync.dma_start(
                    out=g.out_cm[bb][:, col:col + LC].rearrange("(j p) t -> p j t", p=128),
                    in_=o3[ci % 2][:, :].rearrange("p (j t) -> p j t", t=LC)), 16)
            b.sync(_st)
            st3.append(g.sy.n)


def _prep_inputs(kw):
    bf = ml_dtypes.bfloat16
    f32 = np.float32

    def T(a):
        return np.ascontiguousarray(np.asarray(a, f32).T)

    x = np.asarray(kw["L0_attn_x"], f32)
    com = {}
    com["w_inproj"] = T(kw["L0_attn_in_proj_w"])
    com["w_xproj"] = T(kw["L0_attn_x_proj_w"])
    com["w_dtw"] = T(kw["L0_attn_dt_w"])
    com["w_outw"] = T(kw["L0_attn_out_w"])
    com["w_ffg"] = T(kw["L0_attn_ffg_w"])
    com["w_ffp"] = T(kw["L0_attn_ffp_w"])
    com["w_router"] = T(kw["L0_ffn_router_w"])
    com["w_sg"] = T(kw["L0_ffn_sg"])
    com["w_su"] = T(kw["L0_ffn_su"])
    com["w_sd"] = T(kw["L0_ffn_sd"])
    com["w_eg"] = np.ascontiguousarray(np.asarray(kw["L0_ffn_eg"], f32).transpose(0, 2, 1))
    com["w_eu"] = np.ascontiguousarray(np.asarray(kw["L0_ffn_eu"], f32).transpose(0, 2, 1))
    com["w_ed"] = np.ascontiguousarray(np.asarray(kw["L0_ffn_ed"], f32).transpose(0, 2, 1))

    def vtile(v, k):
        v = np.asarray(v, f32).reshape(k, 128)
        return np.ascontiguousarray(v.T)

    com["v_n1g"] = vtile(kw["L0_attn_n1_g"], JT)
    com["v_n1b"] = vtile(kw["L0_attn_n1_b"], JT)
    com["v_n2g"] = vtile(kw["L0_attn_n2_g"], JT)
    com["v_n2b"] = vtile(kw["L0_attn_n2_b"], JT)
    cw = np.asarray(kw["L0_attn_conv_w"], f32).reshape(DI, 4)
    com["v_convw"] = np.ascontiguousarray(
        cw.reshape(IT, 128, 4).transpose(1, 0, 2).reshape(128, IT * 4))
    com["v_convb"] = vtile(kw["L0_attn_conv_b"], IT)
    com["v_dtb"] = vtile(kw["L0_attn_dt_b"], IT)
    com["v_D"] = vtile(kw["L0_attn_D"], IT)
    com["v_ffgb"] = vtile(kw["L0_attn_ffg_b"], 2 * JT)
    com["v_ffpb"] = vtile(kw["L0_attn_ffp_b"], JT)
    al = np.asarray(kw["L0_attn_A_log"], f32)
    com["v_Alog"] = np.ascontiguousarray(
        al.reshape(IT, 128, NS).transpose(1, 0, 2).reshape(128, IT * NS))
    com["c_id"] = np.eye(128, dtype=f32)
    com["c_idf"] = np.eye(128, dtype=f32)
    oh16 = np.zeros((16, 16 * 128), f32)
    for n in range(16):
        oh16[n, n * 128:(n + 1) * 128] = 1
    com["c_oh16"] = oh16
    oh8 = np.zeros((8, 8 * 128), f32)
    for e in range(8):
        oh8[e, e * 128:(e + 1) * 128] = 1
    com["c_oh8"] = oh8

    in_maps = []
    for c in range(8):
        m = dict(com)
        m["xt"] = np.ascontiguousarray(x[2 * c:2 * c + 2].transpose(0, 2, 1))
        in_maps.append(m)
    return in_maps


def kernel(**kw):
    in_maps = _prep_inputs(kw)
    nc = build()
    res = run_bass_kernel_spmd(nc, in_maps, core_ids=list(range(8)))
    out = np.empty((16, L, DM), np.float32)
    for c in range(8):
        o = np.asarray(res.results[c]["out_cm"])
        out[2 * c:2 * c + 2] = o.transpose(0, 2, 1)
    return out


# version 02
# speedup vs baseline: 1.4474x; 1.4474x over previous
import numpy as np
import ml_dtypes
from contextlib import ExitStack

import concourse.bass as bass
import concourse.mybir as mybir
from concourse.alu_op_type import AluOpType as OP
from concourse.bass_utils import run_bass_kernel_spmd

F32 = mybir.dt.float32
BF16 = mybir.dt.bfloat16
AF = mybir.ActivationFunctionType
AX = mybir.AxisListType.X

BC = 2
L = 2048
DM = 1024
DI = 2048
NS = 16
DTR = 64
NE = 8
JT = DM // 128
IT = DI // 128
LC = 512
NCH = L // LC
EPS = 1e-5
BIG = 1e30


class S:

    def __init__(self, ctx, nc, name):
        self.h = ctx.enter_context(nc.semaphore(name))
        self.n = 0

    def inc(self, instr, k=1):
        instr.then_inc(self.h, k)
        self.n += k
        return self.n


class Guard:

    def __init__(self):
        self.edges = []

    def set(self, sem, n):
        self.edges = [(sem.h, n)]

    def add(self, sem, n):
        self.edges.append((sem.h, n))

    def wait(self, eng):
        for h, n in self.edges:
            if n > 0:
                eng.wait_ge(h, n)
        self.edges = []


class G:
    pass


def emit_barrier(g):
    snaps = [(s.h, s.n) for s in [g.st, g.sv, g.ss, g.sp, g.sy, g.sg] if s.n > 0]

    for name in ["tensor", "vector", "scalar", "sync", "gpsimd"]:
        def body(eng):
            for h, n in snaps:
                eng.wait_ge(h, n)
        getattr(g.block, name)(body)


def build(dbg=(), phases=("p1a", "p1b", "p2", "p1c", "p3")):
    nc = bass.Bass("TRN2", target_bir_lowering=False)
    g = G()
    g.nc = nc

    def dram(name, shape, dtype, kind):
        if kind == "Internal" and name in dbg:
            kind = "ExternalOutput"
        return nc.dram_tensor(name, shape, dtype, kind=kind).ap()

    IN = "ExternalInput"
    g.xt = dram("xt", [BC, DM, L], F32, IN)
    g.w_inproj = dram("w_inproj", [DM, 2 * DI], F32, IN)
    g.w_xproj = dram("w_xproj", [DI, 96], F32, IN)
    g.w_dtw = dram("w_dtw", [DTR, DI], F32, IN)
    g.w_outw = dram("w_outw", [DI, DM], F32, IN)
    g.w_ffg = dram("w_ffg", [DM, 2 * DM], F32, IN)
    g.w_ffp = dram("w_ffp", [DM, DM], F32, IN)
    g.w_router = dram("w_router", [DM, NE], F32, IN)
    g.w_sg = dram("w_sg", [DM, DI], F32, IN)
    g.w_su = dram("w_su", [DM, DI], F32, IN)
    g.w_sd = dram("w_sd", [DI, DM], F32, IN)
    g.w_eg = dram("w_eg", [NE, DM, DI], F32, IN)
    g.w_eu = dram("w_eu", [NE, DM, DI], F32, IN)
    g.w_ed = dram("w_ed", [NE, DI, DM], F32, IN)
    g.v_n1g = dram("v_n1g", [128, JT], F32, IN)
    g.v_n1b = dram("v_n1b", [128, JT], F32, IN)
    g.v_n2g = dram("v_n2g", [128, JT], F32, IN)
    g.v_n2b = dram("v_n2b", [128, JT], F32, IN)
    g.v_convw = dram("v_convw", [128, IT * 4], F32, IN)
    g.v_convb = dram("v_convb", [128, IT], F32, IN)
    g.v_dtb = dram("v_dtb", [128, IT], F32, IN)
    g.v_D = dram("v_D", [128, IT], F32, IN)
    g.v_ffgb = dram("v_ffgb", [128, 2 * JT], F32, IN)
    g.v_ffpb = dram("v_ffpb", [128, JT], F32, IN)
    g.v_Alog = dram("v_Alog", [128, IT * NS], F32, IN)
    g.c_id = dram("c_id", [128, 128], F32, IN)
    g.c_idf = dram("c_idf", [128, 128], F32, IN)
    g.c_oh16 = dram("c_oh16", [16, 16 * 128], F32, IN)
    g.c_oh8 = dram("c_oh8", [8, 8 * 128], F32, IN)

    g.out_cm = dram("out_cm", [BC, DM, L], F32, "ExternalOutput")
    g.h_xn = nc.dram_tensor("h_xn", [BC, DM, L], BF16, kind="Internal").ap()
    g.h_xc = nc.dram_tensor("h_xc", [BC, DI, L], BF16, kind="Internal").ap()
    g.h_sz = nc.dram_tensor("h_sz", [BC, DI, L], BF16, kind="Internal").ap()
    g.h_yraw = nc.dram_tensor("h_yraw", [BC, DI, L], BF16, kind="Internal").ap()
    g.h_s = dram("h_s", [BC, DM, L], F32, "Internal")
    g.h_g = nc.dram_tensor("h_g", [BC, DM, L], BF16, kind="Internal").ap()
    g.h_rows = dram("h_rows", [2, BC * L], F32, "Internal")
    g.h_comb = nc.dram_tensor("h_comb", [4, 8, 1024], BF16, kind="Internal").ap()
    g.dbg_mirrors = []
    for nm in dbg:
        srcap = getattr(g, nm, None)
        if srcap is not None and srcap.tensor.dtype == BF16:
            mir = nc.dram_tensor("dbgo_" + nm, list(srcap.shape), F32,
                                 kind="ExternalOutput").ap()
            g.dbg_mirrors.append((srcap, mir))

    with ExitStack() as ctx:
        E = ctx.enter_context
        g.st = S(ctx, nc, "st")
        g.sv = S(ctx, nc, "sv")
        g.ss = S(ctx, nc, "ss")
        g.sp = S(ctx, nc, "sp")
        g.sy = S(ctx, nc, "sy")
        g.sg = S(ctx, nc, "sg")

        g.pb = [E(nc.psum_tensor(f"pb{i}", [128, 512], F32)) for i in range(7)]
        g.pbT = E(nc.psum_tensor("pbT", [128, 256], BF16))

        g.ones_f = E(nc.sbuf_tensor("ones_f", [128, 1], F32))
        g.ones1_f = E(nc.sbuf_tensor("ones1_f", [1, 128], F32))
        g.zeros3 = E(nc.sbuf_tensor("zeros3", [128, 4], BF16))
        g.cEPS = E(nc.sbuf_tensor("cEPS", [128, 1], F32))
        g.c1 = E(nc.sbuf_tensor("c1", [128, 1], F32))
        g.id_sb = E(nc.sbuf_tensor("id_sb", [128, 128], BF16))
        g.idf_sb = E(nc.sbuf_tensor("idf_sb", [128, 128], F32))
        g.oh8_sb = E(nc.sbuf_tensor("oh8_sb", [8, 8 * 128], BF16))
        g.n1g = E(nc.sbuf_tensor("n1g", [128, JT], F32))
        g.n1b = E(nc.sbuf_tensor("n1b", [128, JT], F32))
        g.n2g = E(nc.sbuf_tensor("n2g", [128, JT], F32))
        g.n2b = E(nc.sbuf_tensor("n2b", [128, JT], F32))
        g.convw = E(nc.sbuf_tensor("convw", [128, IT * 4], F32))
        g.convb = E(nc.sbuf_tensor("convb", [128, IT], F32))
        g.dtb = E(nc.sbuf_tensor("dtb", [128, IT], F32))
        g.Dv = E(nc.sbuf_tensor("Dv", [128, IT], F32))
        g.ffgb = E(nc.sbuf_tensor("ffgb", [128, 2 * JT], F32))
        g.ffpb = E(nc.sbuf_tensor("ffpb", [128, JT], F32))
        g.A_sb = E(nc.sbuf_tensor("A_sb", [128, IT * NS], F32))
        g.rt_sb = E(nc.sbuf_tensor("rt_sb", [128, JT * NE], F32))
        g.dt_sb = E(nc.sbuf_tensor("dt_sb", [DTR, BC * L], BF16))
        g.Brow = E(nc.sbuf_tensor("Brow", [NS, BC * L], BF16))
        g.Crow = E(nc.sbuf_tensor("Crow", [NS, BC * L], BF16))
        g.hstate = E(nc.sbuf_tensor("hstate", [128, IT * NS], F32))

        g.block = E(nc.Block())
        b = g.block

        def _setup_dma(sync):
            for dst, src in [(g.n1g, g.v_n1g), (g.n1b, g.v_n1b), (g.n2g, g.v_n2g),
                             (g.n2b, g.v_n2b), (g.convw, g.v_convw), (g.convb, g.v_convb),
                             (g.dtb, g.v_dtb), (g.Dv, g.v_D), (g.ffgb, g.v_ffgb),
                             (g.ffpb, g.v_ffpb), (g.idf_sb, g.c_idf),
                             (g.A_sb, g.v_Alog)]:
                g.sy.inc(sync.dma_start(out=dst[:, :], in_=src), 16)
            g.sy.inc(sync.dma_start(
                out=g.rt_sb[:, :].rearrange("p (k e) -> p k e", e=NE),
                in_=g.w_router.rearrange("(k p) e -> p k e", p=128)), 16)
        b.sync(_setup_dma)
        sd_n = g.sy.n

        def _setup_gdma(gq):
            g.sg.inc(gq.dma_start(out=g.id_sb[:, :], in_=g.c_id), 16)
            g.sg.inc(gq.dma_start(out=g.oh8_sb[:, :], in_=g.c_oh8), 16)
        b.gpsimd(_setup_gdma)
        sg_n = g.sg.n

        def _setup_v(v):
            v.wait_ge(g.sy.h, sd_n)
            v.wait_ge(g.sg.h, sg_n)
            v.memset(g.ones_f[:, :], 1.0)
            v.memset(g.ones1_f[:, :], 1.0)
            v.memset(g.zeros3[:, :], 0.0)
            v.memset(g.cEPS[:, :], EPS)
            v.memset(g.c1[:, :], 1.0)
            g.sv.inc(v.memset(g.hstate[:, :], 0.0))
        b.vector(_setup_v)

        def _setup_s(s):
            s.wait_ge(g.sy.h, sd_n)
            g.ss.inc(s.activation(out=g.A_sb[:, :], in_=g.A_sb[:, :], func=AF.Exp))
        b.scalar(_setup_s)
        ssn = g.ss.n

        def _setup_v2(v):
            v.wait_ge(g.ss.h, ssn)
            g.sv.inc(v.tensor_scalar(out=g.A_sb[:, :], in0=g.A_sb[:, :], scalar1=-1.0,
                                     scalar2=None, op0=OP.mult))
        b.vector(_setup_v2)
        emit_barrier(g)

        if "p1a" in phases:
            emit_p1a(g)
            emit_barrier(g)
        if "p1b" in phases:
            emit_p1b(g)
            emit_barrier(g)
        if "p2" in phases:
            emit_p2(g)
            emit_barrier(g)
        if "p1c" in phases:
            emit_p1c(g)
            emit_barrier(g)
        if "p3" in phases:
            emit_p3(g)

        if g.dbg_mirrors:
            emit_barrier(g)
            emit_dbg_dump(g)

        fin = g.sy.n

        def _fin(sync):
            sync.wait_ge(g.sy.h, fin)
        b.sync(_fin)

    return nc


def emit_dbg_dump(g):
    nc, b = g.nc, g.block
    with ExitStack() as c2:
        E = c2.enter_context
        tb = E(nc.sbuf_tensor("dbg_tb", [128, 2048], BF16))
        tf = E(nc.sbuf_tensor("dbg_tf", [128, 2048], F32))
        prev_cp = 0
        for src, dst in g.dbg_mirrors:
            B_, D_, L_ = src.shape
            PP = min(128, D_)
            for b_ in range(B_):
                for n_ in range(max(1, D_ // 128)):
                    def _in(sync, src=src, b_=b_, n_=n_, prev_cp=prev_cp):
                        if prev_cp:
                            sync.wait_ge(g.ss.h, prev_cp)
                        g.sy.inc(sync.dma_start(
                            out=tb[0:PP, 0:L_],
                            in_=src[b_][n_ * PP:(n_ + 1) * PP, :]), 16)
                    b.sync(_in)
                    lin = g.sy.n

                    def _cp(s, lin=lin):
                        s.wait_ge(g.sy.h, lin)
                        g.ss.inc(s.copy(out=tf[0:PP, 0:L_], in_=tb[0:PP, 0:L_]))
                    b.scalar(_cp)
                    cpn = g.ss.n
                    prev_cp = cpn

                    def _out(sync, dst=dst, b_=b_, n_=n_, cpn=cpn):
                        sync.wait_ge(g.ss.h, cpn)
                        g.sy.inc(sync.dma_start(
                            out=dst[b_][n_ * PP:(n_ + 1) * PP, :],
                            in_=tf[0:PP, 0:L_]), 16)
                    b.sync(_out)


def emit_p1a(g):
    nc, b = g.nc, g.block
    with ExitStack() as c2:
        E = c2.enter_context
        inproj = E(nc.sbuf_tensor("inproj_sl", [128, JT * 2 * DI], BF16))
        xproj = E(nc.sbuf_tensor("xproj_sl", [128, IT * 96], BF16))
        x_sb = E(nc.sbuf_tensor("x_sb", [128, JT * LC], F32))
        bc_rows = E(nc.sbuf_tensor("bc_rows", [128, 4 * LC], F32))
        xn_sb = E(nc.sbuf_tensor("xn_sb", [128, JT * LC], BF16))
        ln1_sb = E(nc.sbuf_tensor("ln1_sb", [128, JT * LC], BF16))
        xin = E(nc.sbuf_tensor("xin", [128, IT * (LC + 3)], BF16))
        sz_sb = E(nc.sbuf_tensor("sz_sb", [128, IT * LC], BF16))
        xc_sb = E(nc.sbuf_tensor("xc_sb", [128, IT * LC], BF16))
        sq_sb = [E(nc.sbuf_tensor(f"sq_sb{i}", [128, LC], F32)) for i in range(2)]
        tmpj = E(nc.sbuf_tensor("tmpj", [128, LC], F32))
        cv = [E(nc.sbuf_tensor(f"cv{i}", [128, LC], F32)) for i in range(2)]
        rtmp = [E(nc.sbuf_tensor(f"rtmp{i}", [1, LC], F32)) for i in range(4)]

        def _wload(gq):
            g.sg.inc(gq.dma_start(
                out=inproj[:, :].rearrange("p (k m) -> p k m", m=2 * DI),
                in_=g.w_inproj.rearrange("(k p) m -> p k m", p=128)), 16)
            g.sg.inc(gq.dma_start(
                out=xproj[:, :].rearrange("p (k m) -> p k m", m=96),
                in_=g.w_xproj.rearrange("(k p) m -> p k m", p=128)), 16)
        b.gpsimd(_wload)
        wl_n = g.sg.n

        chunks = [(bb, cc) for bb in range(BC) for cc in range(NCH)]
        xload = []
        x_free = Guard()
        bank_g = {i: Guard() for i in range(7)}
        hist_g = Guard()
        spill_prev = 0
        xn_st_prev = 0
        sz_st_prev = 0
        xc_st_prev = 0

        for ci, (bb, cc) in enumerate(chunks):
            buf = x_sb
            col = cc * LC

            def _ld(sync, ci=ci, bb=bb, col=col, buf=buf):
                x_free.wait(sync)
                g.sy.inc(sync.dma_start(
                    out=buf[:, :].rearrange("p (j t) -> p j t", t=LC),
                    in_=g.xt[bb][:, col:col + LC].rearrange("(j p) t -> p j t", p=128)), 16)
            b.sync(_ld)
            xload.append(g.sy.n)

            sqv = []
            sqc = []
            for j in range(JT):
                def _sqv(v, j=j, ci=ci, buf=buf):
                    if j == 0:
                        v.wait_ge(g.sy.h, xload[ci])
                    if j >= 2:
                        v.wait_ge(g.st.h, sqc[j - 2])
                    sl = buf[:, j * LC:(j + 1) * LC]
                    sqv.append(g.sv.inc(v.tensor_tensor(
                        out=sq_sb[j % 2][:, :], in0=sl, in1=sl, op=OP.mult)))
                b.vector(_sqv)

                def _stats_j(t, j=j, ci=ci, buf=buf):
                    if j == 0:
                        bank_g[0].wait(t)
                        bank_g[1].wait(t)
                        t.wait_ge(g.sy.h, xload[ci])
                    t.matmul(g.pb[0][0:1, 0:LC], g.ones_f[:, :], buf[:, j * LC:(j + 1) * LC],
                             start=(j == 0), stop=(j == JT - 1))
                    t.wait_ge(g.sv.h, sqv[j])
                    mm = t.matmul(g.pb[1][0:1, 0:LC], g.ones_f[:, :], sq_sb[j % 2][:, :],
                                  start=(j == 0), stop=(j == JT - 1))
                    sqc.append(g.st.inc(mm))
                b.tensor(_stats_j)
            stats_mm = g.st.n

            mu0, msq, var0, vpe = rtmp
            RS0, RS1, RS2, RS3 = 0, LC, 2 * LC, 3 * LC

            def _rowsA(v):
                v.wait_ge(g.st.h, stats_mm)
                v.tensor_scalar(out=mu0[:, :], in0=g.pb[0][0:1, 0:LC], scalar1=1.0 / DM,
                                scalar2=None, op0=OP.mult)
                v.tensor_scalar(out=msq[:, :], in0=g.pb[1][0:1, 0:LC], scalar1=1.0 / DM,
                                scalar2=None, op0=OP.mult)
                v.tensor_tensor(out=var0[:, :], in0=mu0[:, :], in1=mu0[:, :], op=OP.mult)
                v.tensor_tensor(out=var0[:, :], in0=msq[:, :], in1=var0[:, :], op=OP.subtract)
                v.tensor_scalar(out=vpe[:, :], in0=var0[:, :], scalar1=EPS, scalar2=None,
                                op0=OP.add)
                v.reciprocal(out=msq[:, :], in_=vpe[:, :])
                g.sv.inc(v.tensor_tensor(out=var0[:, :], in0=var0[:, :], in1=msq[:, :],
                                         op=OP.mult))
            b.vector(_rowsA)
            rA = g.sv.n
            bank_g[0].set(g.sv, rA)
            bank_g[1].set(g.sv, rA)

            def _rowsB(s, spill_prev=spill_prev):
                s.wait_ge(g.sv.h, rA)
                if spill_prev:
                    s.wait_ge(g.sy.h, spill_prev)
                s.activation(out=vpe[:, :], in_=vpe[:, :], func=AF.Ln)
                s.activation(out=bc_rows[0:1, RS0:RS0 + LC], in_=vpe[:, :], func=AF.Exp,
                             scale=-0.5)
                s.activation(out=var0[:, :], in_=var0[:, :], func=AF.Ln, bias=g.cEPS[0:1, 0:1])
                g.ss.inc(s.activation(out=var0[:, :], in_=var0[:, :], func=AF.Exp, scale=-0.5))
            b.scalar(_rowsB)
            rB = g.ss.n

            def _rowsC(v):
                v.wait_ge(g.ss.h, rB)
                v.tensor_tensor(out=bc_rows[0:1, RS1:RS1 + LC], in0=mu0[:, :],
                                in1=bc_rows[0:1, RS0:RS0 + LC], op=OP.mult)
                v.tensor_tensor(out=bc_rows[0:1, RS2:RS2 + LC],
                                in0=bc_rows[0:1, RS0:RS0 + LC], in1=var0[:, :], op=OP.mult)
                g.sv.inc(v.tensor_tensor(out=bc_rows[0:1, RS3:RS3 + LC],
                                         in0=bc_rows[0:1, RS1:RS1 + LC], in1=var0[:, :],
                                         op=OP.mult))
            b.vector(_rowsC)
            rC = g.sv.n

            def _rowspill(sync, bb=bb, col=col):
                sync.wait_ge(g.sv.h, rC)
                off = bb * L + col
                g.sy.inc(sync.dma_start(out=g.h_rows[0:1, off:off + LC],
                                        in_=bc_rows[0:1, RS0:RS0 + LC]), 16)
                g.sy.inc(sync.dma_start(out=g.h_rows[1:2, off:off + LC],
                                        in_=bc_rows[0:1, RS1:RS1 + LC]), 16)
            b.sync(_rowspill)
            spill_n = g.sy.n
            spill_prev = spill_n

            bc_s = []

            for q in range(4):
                def _bc_t(t, q=q):
                    if q == 0:
                        t.wait_ge(g.sv.h, rC)
                        t.wait_ge(g.sy.h, spill_n)
                    bank_g[2].wait(t)
                    mm = t.matmul(g.pb[2][:, 0:LC], g.ones1_f[:, :],
                                  bc_rows[0:1, q * LC:(q + 1) * LC], start=True, stop=True)
                    bc_t_n.append(g.st.inc(mm))
                bc_t_n = []
                b.tensor(_bc_t)

                def _bc_s(s, q=q):
                    s.wait_ge(g.st.h, bc_t_n[-1])
                    ev = s.copy(out=bc_rows[:, q * LC:(q + 1) * LC], in_=g.pb[2][:, 0:LC])
                    bc_s.append(g.ss.inc(ev))
                b.scalar(_bc_s)
                bank_g[2].set(g.ss, bc_s[-1])
            bc_n = bc_s[-1]

            def _xnl(v, ci=ci, buf=buf, xn_st_prev=xn_st_prev):
                v.wait_ge(g.ss.h, bc_n)
                if xn_st_prev:
                    v.wait_ge(g.sy.h, xn_st_prev)
                last = None
                for j in range(JT):
                    sl = buf[:, j * LC:(j + 1) * LC]
                    v.tensor_tensor(out=tmpj[:, :], in0=sl, in1=bc_rows[:, 0:LC], op=OP.mult)
                    v.tensor_tensor(out=xn_sb[:, j * LC:(j + 1) * LC], in0=tmpj[:, :],
                                    in1=bc_rows[:, LC:2 * LC], op=OP.subtract)
                    v.tensor_tensor(out=tmpj[:, :], in0=sl, in1=bc_rows[:, 2 * LC:3 * LC],
                                    op=OP.mult)
                    v.tensor_tensor(out=tmpj[:, :], in0=tmpj[:, :],
                                    in1=bc_rows[:, 3 * LC:4 * LC], op=OP.subtract)
                    last = v.tensor_scalar(out=ln1_sb[:, j * LC:(j + 1) * LC], in0=tmpj[:, :],
                                           scalar1=g.n1g[:, j:j + 1], scalar2=g.n1b[:, j:j + 1],
                                           op0=OP.mult, op1=OP.add)
                g.sv.inc(last)
            b.vector(_xnl)
            xnl_n = g.sv.n
            x_free.set(g.sv, xnl_n)

            def _xn_store(sync, bb=bb, col=col):
                sync.wait_ge(g.sv.h, xnl_n)
                g.sy.inc(sync.dma_start(
                    out=g.h_xn[bb][:, col:col + LC].rearrange("(j p) t -> p j t", p=128),
                    in_=xn_sb[:, :].rearrange("p (j t) -> p j t", t=LC)), 16)
            b.sync(_xn_store)
            xn_st = g.sy.n
            xn_st_prev = xn_st

            def _hist(s, cc=cc):
                hist_g.wait(s)
                last = None
                for it_ in range(IT):
                    base = it_ * (LC + 3)
                    if cc == 0:
                        last = s.copy(out=xin[:, base:base + 3], in_=g.zeros3[:, 0:3])
                    else:
                        last = s.copy(out=xin[:, base:base + 3],
                                      in_=xin[:, base + LC:base + LC + 3])
                g.ss.inc(last)
            b.scalar(_hist)
            hist_n = g.ss.n

            ipm = []
            ipe = []

            def _ip_t(t, ci=ci):
                t.wait_ge(g.sv.h, xnl_n)
                if ci == 0:
                    t.wait_ge(g.sg.h, wl_n)
                for m in range(2 * IT):
                    bank = 3 + (m % 2)
                    if m >= 2:
                        t.wait_ge(g.ss.h, ipe[m - 2])
                    else:
                        bank_g[bank].wait(t)
                    mm = None
                    for k in range(JT):
                        mm = t.matmul(g.pb[bank][:, 0:LC],
                                      inproj[:, k * 2 * DI + m * 128:k * 2 * DI + (m + 1) * 128],
                                      ln1_sb[:, k * LC:(k + 1) * LC],
                                      start=(k == 0), stop=(k == JT - 1))
                    ipm.append(g.st.inc(mm))
                    _ip_ev(m)

            def _ip_ev(m):
                def _ev(s, m=m, sz_st_prev=sz_st_prev):
                    bank = 3 + (m % 2)
                    s.wait_ge(g.st.h, ipm[m])
                    if m == IT and sz_st_prev:
                        s.wait_ge(g.sy.h, sz_st_prev)
                    if m < IT:
                        base = m * (LC + 3)
                        ev = s.copy(out=xin[:, base + 3:base + 3 + LC], in_=g.pb[bank][:, 0:LC])
                    else:
                        mz = m - IT
                        ev = s.activation(out=sz_sb[:, mz * LC:(mz + 1) * LC],
                                          in_=g.pb[bank][:, 0:LC], func=AF.Silu)
                    ipe.append(g.ss.inc(ev))
                b.scalar(_ev)
            b.tensor(_ip_t)
            bank_g[3].set(g.ss, ipe[-2])
            bank_g[4].set(g.ss, ipe[-1])
            ip_all = ipe[-1]

            def _sz_store(sync, bb=bb, col=col):
                sync.wait_ge(g.ss.h, ip_all)
                g.sy.inc(sync.dma_start(
                    out=g.h_sz[bb][:, col:col + LC].rearrange("(i p) t -> p i t", p=128),
                    in_=sz_sb[:, :].rearrange("p (i t) -> p i t", t=LC)), 16)
            b.sync(_sz_store)
            sz_st_prev = g.sy.n

            convd = []
            silud = []

            def _conv(v):
                for it_ in range(IT):
                    base = it_ * (LC + 3)
                    v.wait_ge(g.ss.h, ipe[it_])
                    v.wait_ge(g.ss.h, hist_n)
                    if it_ >= 2:
                        v.wait_ge(g.ss.h, silud[it_ - 2])
                    cb_ = cv[it_ % 2]
                    v.tensor_scalar(out=cb_[:, :], in0=xin[:, base:base + LC],
                                    scalar1=g.convw[:, it_ * 4:it_ * 4 + 1], scalar2=None,
                                    op0=OP.mult)
                    last = None
                    for k in range(1, 4):
                        last = v.scalar_tensor_tensor(
                            out=cb_[:, :], in0=xin[:, base + k:base + k + LC],
                            scalar=g.convw[:, it_ * 4 + k:it_ * 4 + k + 1],
                            in1=cb_[:, :], op0=OP.mult, op1=OP.add)
                    convd.append(g.sv.inc(last))
                    _silu(it_)

            def _silu(it_):
                def _sl(s, it_=it_, xc_st_prev=xc_st_prev):
                    s.wait_ge(g.sv.h, convd[it_])
                    if it_ == 0 and xc_st_prev:
                        s.wait_ge(g.sy.h, xc_st_prev)
                    sl = s.activation(out=xc_sb[:, it_ * LC:(it_ + 1) * LC],
                                      in_=cv[it_ % 2][:, :], func=AF.Silu,
                                      bias=g.convb[:, it_:it_ + 1])
                    silud.append(g.ss.inc(sl))
                b.scalar(_sl)
            b.vector(_conv)
            hist_g.set(g.sv, convd[-1])
            xc_n = silud[-1]

            def _xc_store(sync, bb=bb, col=col):
                sync.wait_ge(g.ss.h, xc_n)
                g.sy.inc(sync.dma_start(
                    out=g.h_xc[bb][:, col:col + LC].rearrange("(i p) t -> p i t", p=128),
                    in_=xc_sb[:, :].rearrange("p (i t) -> p i t", t=LC)), 16)
            b.sync(_xc_store)
            xc_st_prev = g.sy.n

            xpm = []
            xpe = []

            def _xp_t(t):
                t.wait_ge(g.ss.h, xc_n)
                for gi, (bank, off, w) in enumerate([(5, 0, DTR), (6, DTR, NS), (0, DTR + NS, NS)]):
                    bank_g[bank].wait(t)
                    mm = None
                    for k in range(IT):
                        mm = t.matmul(g.pb[bank][0:w, 0:LC],
                                      xproj[:, k * 96 + off:k * 96 + off + w],
                                      xc_sb[:, k * LC:(k + 1) * LC],
                                      start=(k == 0), stop=(k == IT - 1))
                    xpm.append(g.st.inc(mm))
                    _xp_ev(gi, bank, w)

            def _xp_ev(gi, bank, w, bb=bb, col=col):
                def _ev(s, gi=gi, bank=bank, w=w):
                    s.wait_ge(g.st.h, xpm[gi])
                    dcol = bb * L + col
                    dst = [g.dt_sb, g.Brow, g.Crow][gi]
                    ev = s.copy(out=dst[0:w, dcol:dcol + LC], in_=g.pb[bank][0:w, 0:LC])
                    xpe.append(g.ss.inc(ev))
                b.scalar(_ev)
            b.tensor(_xp_t)
            for gi, bank in enumerate([5, 6, 0]):
                bank_g[bank].set(g.ss, xpe[gi])
            x_free.add(g.st, stats_mm)


def emit_p1b(g):
    nc, b = g.nc, g.block
    with ExitStack() as c2:
        E = c2.enter_context
        dtw = E(nc.sbuf_tensor("dtw_sl", [DTR, DI], BF16))
        oh16_sb = E(nc.sbuf_tensor("oh16_sb", [16, 16 * 128], BF16))
        xc2 = [E(nc.sbuf_tensor(f"xc2_{i}", [128, IT * LC], BF16)) for i in range(2)]
        delta_all = E(nc.sbuf_tensor("delta_all", [128, IT * LC], F32))
        dxc_all = E(nc.sbuf_tensor("dxc_all", [128, IT * LC], BF16))
        y_all = E(nc.sbuf_tensor("y_all", [128, IT * LC], F32))
        yraw = E(nc.sbuf_tensor("yraw_sb", [128, IT * LC], BF16))
        brep = E(nc.sbuf_tensor("brep", [128, LC], BF16))
        crep = E(nc.sbuf_tensor("crep", [128, LC], BF16))
        dA_t = [E(nc.sbuf_tensor(f"dA_t{i}", [128, LC], F32)) for i in range(2)]
        dbu_t = E(nc.sbuf_tensor("dbu_t", [128, LC], BF16))
        h_t = [E(nc.sbuf_tensor(f"h_t{i}", [128, LC], BF16)) for i in range(2)]
        ym_t = [E(nc.sbuf_tensor(f"ym_t{i}", [128, LC], BF16)) for i in range(2)]
        sp_a = [E(nc.sbuf_tensor(f"sp_a{i}", [128, LC], F32)) for i in range(2)]
        sp_b = [E(nc.sbuf_tensor(f"sp_b{i}", [128, LC], F32)) for i in range(2)]

        def _wload(gq):
            g.sg.inc(gq.dma_start(out=dtw[:, :], in_=g.w_dtw), 16)
            g.sg.inc(gq.dma_start(out=oh16_sb[:, :], in_=g.c_oh16), 16)
        b.gpsimd(_wload)
        wl_n = g.sg.n

        chunks = [(bb, cc) for bb in range(BC) for cc in range(NCH)]
        xc_ld = []
        yraw_v = []
        dA_end = []
        pair_dA = []
        pair_v = []
        hcopy = []
        pacc = []
        bank_g = {i: Guard() for i in range(7)}

        for ci, (bb, cc) in enumerate(chunks):
            xcb = xc2[ci % 2]
            col = bb * L + cc * LC

            def _ld(sync, ci=ci, bb=bb, cc=cc):
                if ci >= 2:
                    sync.wait_ge(g.sv.h, yraw_v[ci - 2])
                if ci == 0:
                    sync.wait_ge(g.sg.h, wl_n)
                g.sy.inc(sync.dma_start(
                    out=xc2[ci % 2][:, :].rearrange("p (i t) -> p i t", t=LC),
                    in_=g.h_xc[bb][:, cc * LC:(cc + 1) * LC].rearrange("(i p) t -> p i t", p=128)),
                    16)
            b.sync(_ld)
            xc_ld.append(g.sy.n)

            dtm = []
            spd = []
            spv = []

            def _dt_t(t, ci=ci, col=col):
                for it_ in range(IT):
                    bank = it_ % 2
                    if it_ >= 2:
                        t.wait_ge(g.ss.h, spd[it_ - 2])
                    else:
                        bank_g[bank].wait(t)
                    mm = t.matmul(g.pb[bank][:, 0:LC], dtw[:, it_ * 128:(it_ + 1) * 128],
                                  g.dt_sb[:, col:col + LC], start=True, stop=True)
                    dtm.append(g.st.inc(mm))
                    _sp(it_)

            def _sp(it_, ci=ci):
                def _sp_s(s, it_=it_):
                    bank = it_ % 2
                    s.wait_ge(g.st.h, dtm[it_])
                    if it_ >= 2:
                        s.wait_ge(g.sv.h, spv[it_ - 2])
                    s.activation(out=sp_a[it_ % 2][:, :], in_=g.pb[bank][:, 0:LC], func=AF.Relu,
                                 bias=g.dtb[:, it_:it_ + 1])
                    s.activation(out=sp_b[it_ % 2][:, :], in_=g.pb[bank][:, 0:LC], func=AF.Abs,
                                 bias=g.dtb[:, it_:it_ + 1])
                    s.activation(out=sp_b[it_ % 2][:, :], in_=sp_b[it_ % 2][:, :], func=AF.Exp,
                                 scale=-1.0)
                    sc = s.activation(out=sp_b[it_ % 2][:, :], in_=sp_b[it_ % 2][:, :],
                                      func=AF.Ln, bias=g.c1[0:128, 0:1])
                    spd.append(g.ss.inc(sc))
                b.scalar(_sp_s)

                def _sp_v(v, it_=it_, ci=ci):
                    v.wait_ge(g.ss.h, spd[it_])
                    if it_ == 0:
                        v.wait_ge(g.sy.h, xc_ld[ci])
                        if ci > 0:
                            v.wait_ge(g.ss.h, dA_end[ci - 1])
                    v.tensor_tensor(out=delta_all[:, it_ * LC:(it_ + 1) * LC],
                                    in0=sp_a[it_ % 2][:, :], in1=sp_b[it_ % 2][:, :], op=OP.add)
                    dx = v.tensor_tensor(out=dxc_all[:, it_ * LC:(it_ + 1) * LC],
                                         in0=delta_all[:, it_ * LC:(it_ + 1) * LC],
                                         in1=xcb[:, it_ * LC:(it_ + 1) * LC], op=OP.mult)
                    spv.append(g.sv.inc(dx))
                b.vector(_sp_v)
            b.tensor(_dt_t)
            bank_g[0].set(g.ss, spd[-2])
            bank_g[1].set(g.ss, spd[-1])
            delta_v = spv[-1]

            def _pclear(p, ci=ci):
                if ci > 0:
                    p.wait_ge(g.sv.h, yraw_v[ci - 1])
                g.sp.inc(p.memset(y_all[:, :], 0.0))
            b.gpsimd(_pclear)
            pclear_n = g.sp.n

            for n in range(NS):
                bB = 2 + (n % 2)
                bC = 4 + (n % 2)

                def _rt(t, n=n, bB=bB, bC=bC, col=col):
                    bank_g[bB].wait(t)
                    bank_g[bC].wait(t)
                    t.matmul(g.pb[bB][:, 0:LC], oh16_sb[:, n * 128:(n + 1) * 128],
                             g.Brow[:, col:col + LC], start=True, stop=True)
                    mm = t.matmul(g.pb[bC][:, 0:LC], oh16_sb[:, n * 128:(n + 1) * 128],
                                  g.Crow[:, col:col + LC], start=True, stop=True)
                    rep_t.append(g.st.inc(mm))
                rep_t = []
                b.tensor(_rt)
                rt_n = g.st.n

                def _re(v, n=n, bB=bB, bC=bC):
                    v.wait_ge(g.st.h, rt_n)
                    v.tensor_copy(brep[:, :], g.pb[bB][:, 0:LC])
                    ev = v.tensor_copy(crep[:, :], g.pb[bC][:, 0:LC])
                    rep_s.append(g.sv.inc(ev))
                rep_s = []
                b.vector(_re)
                bank_g[bB].set(g.sv, rep_s[-1])
                bank_g[bC].set(g.sv, rep_s[-1])
                ren = rep_s[-1]

                for it_ in range(IT):
                    kg = len(pair_v)

                    def _dA2(s, n=n, it_=it_, kg=kg, delta_v=delta_v):
                        if kg >= 2:
                            s.wait_ge(g.sv.h, pair_v[kg - 2])
                        s.wait_ge(g.sv.h, delta_v)
                        a = s.activation(out=dA_t[kg % 2][:, :],
                                         in_=delta_all[:, it_ * LC:(it_ + 1) * LC], func=AF.Exp,
                                         scale=g.A_sb[:, it_ * NS + n:it_ * NS + n + 1])
                        pair_dA.append(g.ss.inc(a))
                    b.scalar(_dA2)

                    def _pv(v, n=n, it_=it_, kg=kg, ren=ren):
                        v.wait_ge(g.ss.h, pair_dA[kg])
                        if kg >= 2:
                            v.wait_ge(g.sp.h, hcopy[kg - 2])
                            v.wait_ge(g.sp.h, pacc[kg - 2])
                        v.tensor_tensor(out=dbu_t[:, :],
                                        in0=dxc_all[:, it_ * LC:(it_ + 1) * LC],
                                        in1=brep[:, :], op=OP.mult)
                        v.tensor_tensor_scan(out=h_t[kg % 2][:, :], data0=dA_t[kg % 2][:, :],
                                             data1=dbu_t[:, :],
                                             initial=g.hstate[:, it_ * NS + n:it_ * NS + n + 1],
                                             op0=OP.mult, op1=OP.add)
                        ym = v.tensor_tensor(out=ym_t[kg % 2][:, :], in0=h_t[kg % 2][:, :],
                                             in1=crep[:, :], op=OP.mult)
                        pair_v.append(g.sv.inc(ym))
                    b.vector(_pv)

                    def _hc(p, n=n, it_=it_, kg=kg):
                        p.wait_ge(g.sv.h, pair_v[kg])
                        hc = p.tensor_copy(g.hstate[:, it_ * NS + n:it_ * NS + n + 1],
                                           h_t[kg % 2][:, LC - 1:LC])
                        hcopy.append(g.sp.inc(hc))
                    b.gpsimd(_hc)

                    def _pa(p, it_=it_, kg=kg):
                        p.wait_ge(g.sv.h, pair_v[kg])
                        p.wait_ge(g.sp.h, pclear_n)
                        ac = p.tensor_tensor(out=y_all[:, it_ * LC:(it_ + 1) * LC],
                                             in0=y_all[:, it_ * LC:(it_ + 1) * LC],
                                             in1=ym_t[kg % 2][:, :], op=OP.add)
                        pacc.append(g.sp.inc(ac))
                    b.gpsimd(_pa)
            dA_end.append(pair_dA[-1])

            def _yraw(v, ci=ci):
                v.wait_ge(g.sp.h, pacc[-1])
                v.wait_ge(g.sp.h, hcopy[-1])
                last = None
                for it_ in range(IT):
                    last = v.scalar_tensor_tensor(out=yraw[:, it_ * LC:(it_ + 1) * LC],
                                                  in0=xcb[:, it_ * LC:(it_ + 1) * LC],
                                                  scalar=g.Dv[:, it_:it_ + 1],
                                                  in1=y_all[:, it_ * LC:(it_ + 1) * LC],
                                                  op0=OP.mult, op1=OP.add)
                g.sv.inc(last)
            b.vector(_yraw)
            yraw_v.append(g.sv.n)
            yn = g.sv.n

            def _yst(sync, bb=bb, cc=cc):
                sync.wait_ge(g.sv.h, yn)
                g.sy.inc(sync.dma_start(
                    out=g.h_yraw[bb][:, cc * LC:(cc + 1) * LC].rearrange("(i p) t -> p i t", p=128),
                    in_=yraw[:, :].rearrange("p (i t) -> p i t", t=LC)), 16)
            b.sync(_yst)
            yraw_st = g.sy.n
            if ci + 1 < len(chunks):
                def _yg(v, yraw_st=yraw_st):
                    v.wait_ge(g.sy.h, yraw_st)
                b.vector(_yg)


def emit_p2(g):
    nc, b = g.nc, g.block
    QT = 1024
    NTC = QT // 512
    NHQ = 4
    with ExitStack() as c2:
        E = c2.enter_context
        acc = E(nc.sbuf_tensor("acc_sb", [128, JT * QT], F32))
        eg_sl = [E(nc.sbuf_tensor(f"eg_sl{i}", [128, JT * 512], BF16)) for i in range(2)]
        eu_sl = [E(nc.sbuf_tensor(f"eu_sl{i}", [128, JT * 512], BF16)) for i in range(2)]
        ed_sl = [E(nc.sbuf_tensor(f"ed_sl{i}", [128, 4 * DM], BF16)) for i in range(4)]
        xn_tc = [E(nc.sbuf_tensor(f"xn_tc{i}", [128, JT * 512], BF16)) for i in range(2)]
        xf_sb = E(nc.sbuf_tensor("xf_sb", [128, JT * 512], F32))
        rowseg = E(nc.sbuf_tensor("rowseg", [1, 2 * 512], F32))
        bc2 = E(nc.sbuf_tensor("bc2", [128, 2 * 512], F32))
        hid = E(nc.sbuf_tensor("hid", [128, NTC * IT * 512], BF16))
        sgate = E(nc.sbuf_tensor("sgate", [128, 512], BF16))
        cb_sb = E(nc.sbuf_tensor("cb_sb", [128, QT], BF16))
        comb_cm = E(nc.sbuf_tensor("comb_cm", [8, QT], BF16))
        lgq = E(nc.sbuf_tensor("lgq", [8, 512], F32))
        ex_sb = E(nc.sbuf_tensor("ex_sb", [128, 8], F32))
        eq1 = E(nc.sbuf_tensor("eq1", [128, 8], F32))
        eq2 = E(nc.sbuf_tensor("eq2", [128, 8], F32))
        msk = E(nc.sbuf_tensor("msk", [128, 8], F32))
        colv = [E(nc.sbuf_tensor(f"colv{i}", [128, 1], F32)) for i in range(6)]
        comb_tm = E(nc.sbuf_tensor("comb_tm", [128, 8], BF16))

        bank_g = {i: Guard() for i in range(7)}
        bankT_g = Guard()
        eg_guard = [Guard(), Guard()]
        ed_guard = [Guard() for _ in range(4)]
        hid_guard = Guard()
        acc_free = Guard()
        comb_war = Guard()
        cb_war = Guard()
        lgq_g = Guard()
        xn_guard = [Guard(), Guard()]
        gst_guard = Guard()
        ed_ld = {}
        ct_hist = []
        hid_done = {}
        units = [(ve, hq) for ve in range(NE + 1) for hq in range(NHQ)]

        for q in range(4):
            bb = q // 2
            tok0 = (q % 2) * QT

            cmb_ev = []
            for rc in range(NTC):
                colq = tok0 + rc * 512

                def _rld(sync, bb=bb, colq=colq):
                    sync.wait_ge(g.sv.h, g.sv.n)
                    g.sy.inc(sync.dma_start(
                        out=xf_sb[:, :].rearrange("p (j t) -> p j t", t=512),
                        in_=g.xt[bb][:, colq:colq + 512].rearrange("(j p) t -> p j t", p=128)), 16)
                    off = bb * L + colq
                    g.sy.inc(sync.dma_start(out=rowseg[:, 0:512],
                                            in_=g.h_rows[0:1, off:off + 512]), 16)
                    g.sy.inc(sync.dma_start(out=rowseg[:, 512:1024],
                                            in_=g.h_rows[1:2, off:off + 512]), 16)
                b.sync(_rld)
                rld = g.sy.n

                bcm = []
                bce = []
                for qq in range(2):
                    def _bt(t, qq=qq):
                        if qq == 0:
                            t.wait_ge(g.sy.h, rld)
                        bank_g[0].wait(t)
                        mm = t.matmul(g.pb[0][:, 0:512], g.ones1_f[:, :],
                                      rowseg[:, qq * 512:(qq + 1) * 512], start=True, stop=True)
                        bcm.append(g.st.inc(mm))
                    b.tensor(_bt)

                    def _bs(v, qq=qq):
                        v.wait_ge(g.st.h, bcm[-1])
                        ev = v.tensor_copy(bc2[:, qq * 512:(qq + 1) * 512], g.pb[0][:, 0:512])
                        bce.append(g.sv.inc(ev))
                    b.vector(_bs)
                    bank_g[0].set(g.sv, bce[-1])
                bcn = bce[-1]

                def _rxn(v):
                    v.wait_ge(g.sy.h, rld)
                    last = None
                    for j in range(JT):
                        sl = xf_sb[:, j * 512:(j + 1) * 512]
                        v.tensor_tensor(out=sl, in0=sl, in1=bc2[:, 0:512], op=OP.mult)
                        last = v.tensor_tensor(out=sl, in0=sl, in1=bc2[:, 512:1024],
                                               op=OP.subtract)
                    g.sv.inc(last)
                b.vector(_rxn)
                xnf_n = g.sv.n

                lgm = []

                def _lg(t):
                    t.wait_ge(g.sv.h, xnf_n)
                    bank_g[1].wait(t)
                    mm = None
                    for k in range(JT):
                        mm = t.matmul(g.pb[1][0:8, 0:512], g.rt_sb[:, k * NE:(k + 1) * NE],
                                      xf_sb[:, k * 512:(k + 1) * 512],
                                      start=(k == 0), stop=(k == JT - 1))
                    lgm.append(g.st.inc(mm))
                b.tensor(_lg)

                def _lge(v):
                    v.wait_ge(g.st.h, lgm[-1])
                    lgq_g.wait(v)
                    g.sv.inc(v.tensor_copy(lgq[:, :], g.pb[1][0:8, 0:512]))
                b.vector(_lge)
                lgen = g.sv.n
                bank_g[1].set(g.sv, lgen)

                for tt in range(4):
                    tpm = []

                    def _tp(t, tt=tt):
                        t.wait_ge(g.sv.h, lgen)
                        bank_g[2].wait(t)
                        mm = t.transpose(g.pb[2][0:128, 0:8], lgq[0:8, tt * 128:(tt + 1) * 128],
                                         g.idf_sb[0:8, 0:8])
                        tpm.append(g.st.inc(mm))
                    b.tensor(_tp)
                    prev_ctn = ct_hist[-1] if ct_hist else 0
                    tpn = g.st.n
                    lgq_g.add(g.st, tpn)

                    def _sm1(v):
                        v.wait_ge(g.st.h, tpn)
                        v.tensor_reduce(out=colv[0][:, :], in_=g.pb[2][0:128, 0:8], axis=AX,
                                        op=OP.max)
                        v.drain()
                        g.sv.inc(v.tensor_scalar(out=colv[1][:, :], in0=colv[0][:, :],
                                                 scalar1=-1.0, scalar2=None, op0=OP.mult))
                    b.vector(_sm1)
                    smn = g.sv.n

                    def _sm2(s):
                        s.wait_ge(g.sv.h, smn)
                        g.ss.inc(s.activation(out=ex_sb[:, :], in_=g.pb[2][0:128, 0:8],
                                              func=AF.Exp, bias=colv[1][:, 0:1]))
                    b.scalar(_sm2)
                    exn = g.ss.n
                    bank_g[2].set(g.ss, exn)

                    def _top2(v, prev_ctn=prev_ctn):
                        v.wait_ge(g.ss.h, exn)
                        if prev_ctn:
                            v.wait_ge(g.st.h, prev_ctn)
                        v.tensor_reduce(out=colv[2][:, :], in_=ex_sb[:, :], axis=AX, op=OP.max)
                        v.drain()
                        v.tensor_scalar(out=eq1[:, :], in0=ex_sb[:, :], scalar1=colv[2][:, 0:1],
                                        scalar2=None, op0=OP.is_equal)
                        v.drain()
                        v.tensor_scalar(out=msk[:, :], in0=eq1[:, :], scalar1=BIG, scalar2=None,
                                        op0=OP.mult)
                        v.drain()
                        v.tensor_tensor(out=msk[:, :], in0=ex_sb[:, :], in1=msk[:, :],
                                        op=OP.subtract)
                        v.drain()
                        v.tensor_reduce(out=colv[3][:, :], in_=msk[:, :], axis=AX, op=OP.max)
                        v.drain()
                        v.tensor_tensor(out=colv[4][:, :], in0=colv[2][:, :], in1=colv[3][:, :],
                                        op=OP.add)
                        v.drain()
                        v.reciprocal(out=colv[5][:, :], in_=colv[4][:, :])
                        v.drain()
                        v.tensor_tensor(out=colv[4][:, :], in0=colv[2][:, :], in1=colv[5][:, :],
                                        op=OP.mult)
                        v.tensor_tensor(out=colv[5][:, :], in0=colv[3][:, :], in1=colv[5][:, :],
                                        op=OP.mult)
                        v.drain()
                        v.tensor_scalar(out=eq2[:, :], in0=msk[:, :], scalar1=colv[3][:, 0:1],
                                        scalar2=None, op0=OP.is_equal)
                        v.drain()
                        v.tensor_scalar(out=eq2[:, :], in0=eq2[:, :], scalar1=colv[5][:, 0:1],
                                        scalar2=None, op0=OP.mult)
                        v.drain()
                        tm = v.scalar_tensor_tensor(out=comb_tm[:, :], in0=eq1[:, :],
                                                    scalar=colv[4][:, 0:1], in1=eq2[:, :],
                                                    op0=OP.mult, op1=OP.add)
                        g.sv.inc(tm)
                    b.vector(_top2)
                    t2n = g.sv.n

                    ctm = []

                    def _ct(t):
                        t.wait_ge(g.sv.h, t2n)
                        bankT_g.wait(t)
                        mm = t.transpose(g.pbT[0:8, 0:128], comb_tm[:, 0:8], g.id_sb[:, :])
                        ctm.append(g.st.inc(mm))
                    b.tensor(_ct)
                    ctn = g.st.n
                    ct_hist.append(ctn)

                    def _ce(v, tt=tt, rc=rc):
                        v.wait_ge(g.st.h, ctn)
                        comb_war.wait(v)
                        ev = v.tensor_copy(comb_cm[0:8, rc * 512 + tt * 128:rc * 512 + (tt + 1) * 128],
                                           g.pbT[0:8, 0:128])
                        cmb_ev.append(g.sv.inc(ev))
                    b.vector(_ce)
                    bankT_g.set(g.sv, cmb_ev[-1])
            comb_n = cmb_ev[-1]

            def _comb_dump(sync, q=q):
                sync.wait_ge(g.sv.h, comb_n)
                g.sy.inc(sync.dma_start(out=g.h_comb[q], in_=comb_cm[0:8, :]), 16)
            b.sync(_comb_dump)

            shared_acc_ref = [0]
            cur_cbn = 0
            for ui, (ve, hq) in enumerate(units):
                gui = q * len(units) + ui
                if ve == 0:
                    sg_, su_, sd_ = g.w_sg, g.w_su, g.w_sd
                else:
                    sg_, su_, sd_ = g.w_eg[ve - 1], g.w_eu[ve - 1], g.w_ed[ve - 1]
                hc0 = hq * 512
                egb = eg_sl[gui % 2]
                eub = eu_sl[gui % 2]
                edb = ed_sl[hq]

                def _slab(gq, sg_=sg_, su_=su_, sd_=sd_, hc0=hc0, egb=egb, eub=eub, edb=edb,
                          gui=gui, hq=hq):
                    eg_guard[gui % 2].wait(gq)
                    g.sg.inc(gq.dma_start(
                        out=egb[:, :].rearrange("p (k m) -> p k m", m=512),
                        in_=sg_[:, hc0:hc0 + 512].rearrange("(k p) m -> p k m", p=128)), 16)
                    g.sg.inc(gq.dma_start(
                        out=eub[:, :].rearrange("p (k m) -> p k m", m=512),
                        in_=su_[:, hc0:hc0 + 512].rearrange("(k p) m -> p k m", p=128)), 16)
                    ed_guard[hq].wait(gq)
                    g.sg.inc(gq.dma_start(
                        out=edb[:, :].rearrange("p (k m) -> p k m", m=DM),
                        in_=sd_[hc0:hc0 + 512, :].rearrange("(k p) m -> p k m", p=128)), 16)
                b.gpsimd(_slab)
                slab_n = g.sg.n
                ed_ld[(ve, hq)] = slab_n

                if ve >= 1 and hq == 0:
                    cbm = []
                    cbe = []
                    for tc in range(NTC):
                        def _cb(t, ve=ve, tc=tc):
                            t.wait_ge(g.sv.h, comb_n)
                            bank_g[0].wait(t)
                            mm = t.matmul(g.pb[0][:, 0:512],
                                          g.oh8_sb[:, (ve - 1) * 128:ve * 128],
                                          comb_cm[0:8, tc * 512:(tc + 1) * 512],
                                          start=True, stop=True)
                            cbm.append(g.st.inc(mm))
                        b.tensor(_cb)
                        comb_war.add(g.st, g.st.n)

                        def _cbe(s, tc=tc):
                            s.wait_ge(g.st.h, cbm[-1])
                            if tc == 0:
                                cb_war.wait(s)
                            ev = s.copy(out=cb_sb[:, tc * 512:(tc + 1) * 512],
                                        in_=g.pb[0][:, 0:512])
                            cbe.append(g.ss.inc(ev))
                        b.scalar(_cbe)
                        bank_g[0].set(g.ss, cbe[-1])
                    cur_cbn = cbe[-1]
                cbn = cur_cbn if ve >= 1 else 0

                for tc in range(NTC):
                    def _xnld(sync, tc=tc, bb=bb, tok0=tok0):
                        xn_guard[tc % 2].wait(sync)
                        colq = tok0 + tc * 512
                        g.sy.inc(sync.dma_start(
                            out=xn_tc[tc % 2][:, :].rearrange("p (j t) -> p j t", t=512),
                            in_=g.h_xn[bb][:, colq:colq + 512].rearrange(
                                "(j p) t -> p j t", p=128)), 16)
                    b.sync(_xnld)
                    xnld_n = g.sy.n

                    gm = []
                    um = []
                    ge = []
                    hv = []

                    def _hid_one(m, tc=tc, hq=hq, ve=ve, cbn=cbn):
                        bg_ = 2 + (m % 2)
                        bu_ = 4 + (m % 2)

                        def _sev(s, m=m):
                            s.wait_ge(g.st.h, gm[m])
                            ev = s.activation(out=sgate[:, :], in_=g.pb[bg_][:, 0:512],
                                              func=AF.Silu)
                            ge.append(g.ss.inc(ev))
                        b.scalar(_sev)
                        sevn = g.ss.n

                        def _hvf(v, m=m):
                            v.wait_ge(g.ss.h, sevn)
                            v.wait_ge(g.st.h, um[m])
                            if m == 0:
                                hid_guard.wait(v)
                            mh = (tc * IT) + hq * 4 + m
                            sl = hid[:, mh * 512:(mh + 1) * 512]
                            t1 = v.tensor_tensor(out=sl, in0=sgate[:, :],
                                                 in1=g.pb[bu_][:, 0:512], op=OP.mult)
                            if ve >= 1:
                                if cbn:
                                    v.wait_ge(g.ss.h, cbn)
                                t1 = v.tensor_tensor(out=sl, in0=sl,
                                                     in1=cb_sb[:, tc * 512:(tc + 1) * 512],
                                                     op=OP.mult)
                            hv.append(g.sv.inc(t1))
                        b.vector(_hvf)

                    def _gu(t, tc=tc, egb=egb, eub=eub, slab_n=slab_n, xnld_n=xnld_n):
                        t.wait_ge(g.sg.h, slab_n)
                        t.wait_ge(g.sy.h, xnld_n)
                        xb = xn_tc[tc % 2]
                        for m in range(4):
                            bg_ = 2 + (m % 2)
                            bu_ = 4 + (m % 2)
                            if m >= 2:
                                t.wait_ge(g.ss.h, ge[m - 2])
                                t.wait_ge(g.sv.h, hv[m - 2])
                            else:
                                bank_g[bg_].wait(t)
                                bank_g[bu_].wait(t)
                            mm = None
                            for k in range(JT):
                                mm = t.matmul(g.pb[bg_][:, 0:512],
                                              egb[:, k * 512 + m * 128:k * 512 + (m + 1) * 128],
                                              xb[:, k * 512:(k + 1) * 512],
                                              start=(k == 0), stop=(k == JT - 1))
                            gm.append(g.st.inc(mm))
                            for k in range(JT):
                                mm = t.matmul(g.pb[bu_][:, 0:512],
                                              eub[:, k * 512 + m * 128:k * 512 + (m + 1) * 128],
                                              xb[:, k * 512:(k + 1) * 512],
                                              start=(k == 0), stop=(k == JT - 1))
                            um.append(g.st.inc(mm))
                            _hid_one(m)
                    b.tensor(_gu)
                    bank_g[2].set(g.ss, ge[-2])
                    bank_g[3].set(g.ss, ge[-1])
                    bank_g[4].set(g.sv, hv[-2])
                    bank_g[5].set(g.sv, hv[-1])
                    xn_guard[tc % 2].set(g.st, g.st.n)
                    if hq == NHQ - 1:
                        hid_done[tc] = hv[-1]

                        dnm = []

                        def _dn(t, tc=tc, ve=ve):
                            for hh in range(NHQ):
                                t.wait_ge(g.sg.h, ed_ld[(ve, hh)])
                            t.wait_ge(g.sv.h, hid_done[tc])
                            for mo in range(JT):
                                dbk = 6 if mo % 2 == 0 else 1
                                bank_g[dbk].wait(t)
                                mm = None
                                for hh in range(NHQ):
                                    for kl in range(4):
                                        mh = (tc * IT) + hh * 4 + kl
                                        mm = t.matmul(
                                            g.pb[dbk][:, 0:512],
                                            ed_sl[hh][:, kl * DM + mo * 128:
                                                      kl * DM + (mo + 1) * 128],
                                            hid[:, mh * 512:(mh + 1) * 512],
                                            start=(hh == 0 and kl == 0),
                                            stop=(hh == NHQ - 1 and kl == 3))
                                dnm.append(g.st.inc(mm))
                                _acc_one(mo)

                        def _acc_one(mo, tc=tc, ve=ve):
                            sl_lo = mo * QT + tc * 512
                            dbk = 6 if mo % 2 == 0 else 1

                            def _as(s, mo=mo, dbk=dbk):
                                s.wait_ge(g.st.h, dnm[mo])
                                acc_free.wait(s)
                                ac = s.copy(out=acc[:, sl_lo:sl_lo + 512],
                                            in_=g.pb[dbk][:, 0:512])
                                bank_g[dbk].set(g.ss, g.ss.inc(ac))

                            def _av(v, mo=mo, dbk=dbk):
                                v.wait_ge(g.st.h, dnm[mo])
                                v.wait_ge(g.ss.h, shared_acc_ref[0])
                                ac = v.tensor_tensor(out=acc[:, sl_lo:sl_lo + 512],
                                                     in0=acc[:, sl_lo:sl_lo + 512],
                                                     in1=g.pb[dbk][:, 0:512], op=OP.add)
                                bank_g[dbk].set(g.sv, g.sv.inc(ac))
                            if ve == 0:
                                b.scalar(_as)
                            else:
                                b.vector(_av)
                        b.tensor(_dn)
                        if ve == 0 and tc == NTC - 1:
                            shared_acc_ref[0] = g.ss.n
                eg_guard[gui % 2].set(g.st, g.st.n)
                if hq == NHQ - 1:
                    dn_end = g.st.n
                    for hh in range(NHQ):
                        ed_guard[hh].set(g.st, dn_end)
                    hid_guard.set(g.st, dn_end)
                    cb_war.set(g.sv, g.sv.n)

            for tc in range(NTC):
                gss = []

                gbuf = xn_tc[tc % 2]

                def _gs(s, tc=tc, gbuf=gbuf):
                    s.wait_ge(g.sv.h, g.sv.n)
                    gst_guard.wait(s)
                    last = None
                    for mo in range(JT):
                        last = s.activation(
                            out=gbuf[:, mo * 512:(mo + 1) * 512],
                            in_=acc[:, mo * QT + tc * 512:mo * QT + (tc + 1) * 512],
                            func=AF.Silu)
                    gss.append(g.ss.inc(last))
                b.scalar(_gs)

                def _gst(sync, tc=tc, bb=bb, tok0=tok0, gbuf=gbuf):
                    sync.wait_ge(g.ss.h, gss[-1])
                    colq = tok0 + tc * 512
                    g.sy.inc(sync.dma_start(
                        out=g.h_g[bb][:, colq:colq + 512].rearrange("(j p) t -> p j t", p=128),
                        in_=gbuf[:, :].rearrange("p (j t) -> p j t", t=512)), 16)
                b.sync(_gst)
                gst_guard.set(g.sy, g.sy.n)
                xn_guard[tc % 2].add(g.sy, g.sy.n)
            acc_free.set(g.ss, g.ss.n)


def emit_p1c(g):
    nc, b = g.nc, g.block
    with ExitStack() as c2:
        E = c2.enter_context
        outw = E(nc.sbuf_tensor("outw_sl", [128, IT * DM], BF16))
        ffg = E(nc.sbuf_tensor("ffg_sl", [128, JT * 2 * DM], BF16))
        ffp = E(nc.sbuf_tensor("ffp_sl", [128, JT * DM], BF16))
        yraw2 = E(nc.sbuf_tensor("yraw2", [128, IT * LC], BF16))
        sz2 = E(nc.sbuf_tensor("sz2", [128, IT * LC], BF16))
        xf2 = E(nc.sbuf_tensor("xf2", [128, JT * LC], F32))
        sq2 = [E(nc.sbuf_tensor(f"sq2_{i}", [128, LC], F32)) for i in range(2)]
        tmp2 = E(nc.sbuf_tensor("tmp2", [128, LC], F32))
        ln2 = E(nc.sbuf_tensor("ln2_sb", [128, JT * LC], BF16))
        sg2 = E(nc.sbuf_tensor("sg2", [128, JT * LC], BF16))
        ff2 = E(nc.sbuf_tensor("ff2", [128, JT * LC], BF16))
        rowseg = E(nc.sbuf_tensor("rowseg2", [1, 2 * LC], F32))
        bc3 = E(nc.sbuf_tensor("bc3", [128, 4 * LC], F32))
        rtmp = [E(nc.sbuf_tensor(f"rtmpc{i}", [1, LC], F32)) for i in range(2)]

        def _wload(gq):
            g.sg.inc(gq.dma_start(
                out=outw[:, :].rearrange("p (k m) -> p k m", m=DM),
                in_=g.w_outw.rearrange("(k p) m -> p k m", p=128)), 16)
            g.sg.inc(gq.dma_start(
                out=ffg[:, :].rearrange("p (k m) -> p k m", m=2 * DM),
                in_=g.w_ffg.rearrange("(k p) m -> p k m", p=128)), 16)
            g.sg.inc(gq.dma_start(
                out=ffp[:, :].rearrange("p (k m) -> p k m", m=DM),
                in_=g.w_ffp.rearrange("(k p) m -> p k m", p=128)), 16)
        b.gpsimd(_wload)
        wl_n = g.sg.n

        chunks = [(bb, cc) for bb in range(BC) for cc in range(NCH)]
        lds = []
        sst = []
        chunk_done = []
        bank_g = {i: Guard() for i in range(7)}
        act_g = Guard()

        for ci, (bb, cc) in enumerate(chunks):
            yb, zb, xb = yraw2, sz2, xf2

            def _ld(sync, ci=ci, bb=bb, cc=cc):
                if ci >= 1:
                    sync.wait_ge(g.sv.h, chunk_done[ci - 1])
                    sync.wait_ge(g.sy.h, sst[ci - 1])
                col = cc * LC
                g.sy.inc(sync.dma_start(
                    out=yraw2[:, :].rearrange("p (i t) -> p i t", t=LC),
                    in_=g.h_yraw[bb][:, col:col + LC].rearrange("(i p) t -> p i t", p=128)), 16)
                g.sy.inc(sync.dma_start(
                    out=sz2[:, :].rearrange("p (i t) -> p i t", t=LC),
                    in_=g.h_sz[bb][:, col:col + LC].rearrange("(i p) t -> p i t", p=128)), 16)
                g.sy.inc(sync.dma_start(
                    out=xf2[:, :].rearrange("p (j t) -> p j t", t=LC),
                    in_=g.xt[bb][:, col:col + LC].rearrange("(j p) t -> p j t", p=128)), 16)
                off = bb * L + col
                g.sy.inc(sync.dma_start(out=rowseg[:, 0:LC],
                                        in_=g.h_rows[0:1, off:off + LC]), 16)
                g.sy.inc(sync.dma_start(out=rowseg[:, LC:2 * LC],
                                        in_=g.h_rows[1:2, off:off + LC]), 16)
            b.sync(_ld)
            lds.append(g.sy.n)

            def _yf(v, ci=ci):
                v.wait_ge(g.sy.h, lds[ci])
                act_g.wait(v)
                last = None
                for it_ in range(IT):
                    last = v.tensor_tensor(out=yb[:, it_ * LC:(it_ + 1) * LC],
                                           in0=yb[:, it_ * LC:(it_ + 1) * LC],
                                           in1=zb[:, it_ * LC:(it_ + 1) * LC], op=OP.mult)
                g.sv.inc(last)
            b.vector(_yf)
            yf_n = g.sv.n

            bcm = []
            bce = []
            for qq in range(2):
                def _bt(t, qq=qq, ci=ci):
                    if qq == 0:
                        t.wait_ge(g.sy.h, lds[ci])
                    bank_g[6].wait(t)
                    mm = t.matmul(g.pb[6][:, 0:LC], g.ones1_f[:, :],
                                  rowseg[:, qq * LC:(qq + 1) * LC], start=True, stop=True)
                    bcm.append(g.st.inc(mm))
                b.tensor(_bt)

                def _bs(s, qq=qq):
                    s.wait_ge(g.st.h, bcm[-1])
                    ev = s.copy(out=bc3[:, qq * LC:(qq + 1) * LC], in_=g.pb[6][:, 0:LC])
                    bce.append(g.ss.inc(ev))
                b.scalar(_bs)
                bank_g[6].set(g.ss, bce[-1])
            bcn = bce[-1]

            opm = []
            x2v = []

            def _op(t, ci=ci):
                t.wait_ge(g.sv.h, yf_n)
                if ci == 0:
                    t.wait_ge(g.sg.h, wl_n)
                for mo in range(JT):
                    bank = mo % 2
                    if mo >= 2:
                        t.wait_ge(g.sv.h, x2v[mo - 2])
                    else:
                        bank_g[bank].wait(t)
                    mm = None
                    for k in range(IT):
                        mm = t.matmul(g.pb[bank][:, 0:LC],
                                      outw[:, k * DM + mo * 128:k * DM + (mo + 1) * 128],
                                      yb[:, k * LC:(k + 1) * LC],
                                      start=(k == 0), stop=(k == IT - 1))
                    opm.append(g.st.inc(mm))
                    _x2_one(mo)

            def _x2_one(mo):
                def _xv(v, mo=mo):
                    bank = mo % 2
                    v.wait_ge(g.st.h, opm[mo])
                    v.wait_ge(g.ss.h, bcn)
                    sl = xb[:, mo * LC:(mo + 1) * LC]
                    v.tensor_tensor(out=sl, in0=sl, in1=bc3[:, 0:LC], op=OP.mult)
                    v.tensor_tensor(out=sl, in0=sl, in1=bc3[:, LC:2 * LC], op=OP.subtract)
                    xv = v.tensor_tensor(out=sl, in0=sl, in1=g.pb[bank][:, 0:LC], op=OP.add)
                    x2v.append(g.sv.inc(xv))
                b.vector(_xv)
            b.tensor(_op)
            bank_g[0].set(g.sv, x2v[-2])
            bank_g[1].set(g.sv, x2v[-1])
            x2n = x2v[-1]

            sqv = []
            sqc = []
            for j in range(JT):
                def _sqv(v, j=j):
                    v.wait_ge(g.sv.h, x2v[j])
                    if j >= 2:
                        v.wait_ge(g.st.h, sqc[j - 2])
                    sq = v.tensor_tensor(out=sq2[j % 2][:, :], in0=xb[:, j * LC:(j + 1) * LC],
                                         in1=xb[:, j * LC:(j + 1) * LC], op=OP.mult)
                    sqv.append(g.sv.inc(sq))
                b.vector(_sqv)

                def _st2(t, j=j):
                    if j == 0:
                        bank_g[2].wait(t)
                        bank_g[3].wait(t)
                    t.wait_ge(g.sv.h, x2v[j])
                    t.matmul(g.pb[2][0:1, 0:LC], g.ones_f[:, :], xb[:, j * LC:(j + 1) * LC],
                             start=(j == 0), stop=(j == JT - 1))
                    t.wait_ge(g.sv.h, sqv[j])
                    mm = t.matmul(g.pb[3][0:1, 0:LC], g.ones_f[:, :], sq2[j % 2][:, :],
                                  start=(j == 0), stop=(j == JT - 1))
                    sqc.append(g.st.inc(mm))
                b.tensor(_st2)
            stn = g.st.n

            mu2, msq2 = rowseg[0:1, 0:LC], rowseg[0:1, LC:2 * LC]
            var2, vpe2 = rtmp[0][:, :], rtmp[1][:, :]

            def _r2A(v):
                v.wait_ge(g.st.h, stn)
                v.tensor_scalar(out=mu2, in0=g.pb[2][0:1, 0:LC], scalar1=1.0 / DM,
                                scalar2=None, op0=OP.mult)
                v.tensor_scalar(out=msq2, in0=g.pb[3][0:1, 0:LC], scalar1=1.0 / DM,
                                scalar2=None, op0=OP.mult)
                v.tensor_tensor(out=var2, in0=mu2, in1=mu2, op=OP.mult)
                g.sv.inc(v.tensor_tensor(out=var2, in0=msq2, in1=var2,
                                         op=OP.subtract))
            b.vector(_r2A)
            r2a = g.sv.n
            bank_g[2].set(g.sv, r2a)
            bank_g[3].set(g.sv, r2a)

            def _r2B(s):
                s.wait_ge(g.sv.h, r2a)
                s.activation(out=vpe2, in_=var2, func=AF.Ln, bias=g.cEPS[0:1, 0:1])
                g.ss.inc(s.activation(out=vpe2, in_=vpe2, func=AF.Exp, scale=-0.5))
            b.scalar(_r2B)
            r2b = g.ss.n

            def _r2C(v):
                v.wait_ge(g.ss.h, r2b)
                g.sv.inc(v.tensor_tensor(out=mu2, in0=mu2, in1=vpe2,
                                         op=OP.mult))
            b.vector(_r2C)
            r2c = g.sv.n

            b2m = []
            b2e = []
            for qq, src in [(0, vpe2), (1, mu2)]:
                def _b2t(t, qq=qq, src=src):
                    t.wait_ge(g.sv.h, r2c)
                    bank_g[6].wait(t)
                    mm = t.matmul(g.pb[6][:, 0:LC], g.ones1_f[:, :], src[:, :],
                                  start=True, stop=True)
                    b2m.append(g.st.inc(mm))
                b.tensor(_b2t)

                def _b2s(s, qq=qq):
                    s.wait_ge(g.st.h, b2m[-1])
                    ev = s.copy(out=bc3[:, (2 + qq) * LC:(3 + qq) * LC], in_=g.pb[6][:, 0:LC])
                    b2e.append(g.ss.inc(ev))
                b.scalar(_b2s)
                bank_g[6].set(g.ss, b2e[-1])
            b2n = b2e[-1]

            def _ln2(v):
                v.wait_ge(g.ss.h, b2n)
                last = None
                for j in range(JT):
                    v.tensor_tensor(out=tmp2[:, :], in0=xb[:, j * LC:(j + 1) * LC],
                                    in1=bc3[:, 2 * LC:3 * LC], op=OP.mult)
                    v.tensor_tensor(out=tmp2[:, :], in0=tmp2[:, :], in1=bc3[:, 3 * LC:4 * LC],
                                    op=OP.subtract)
                    last = v.tensor_scalar(out=ln2[:, j * LC:(j + 1) * LC], in0=tmp2[:, :],
                                           scalar1=g.n2g[:, j:j + 1], scalar2=g.n2b[:, j:j + 1],
                                           op0=OP.mult, op1=OP.add)
                g.sv.inc(last)
            b.vector(_ln2)
            ln2n = g.sv.n

            fgm = []
            fge = []

            def _ffg(t, ci=ci):
                t.wait_ge(g.sv.h, ln2n)
                for m in range(2 * JT):
                    bank = 2 + (m % 2)
                    if m >= 2:
                        eng, cnt = fge[m - 2]
                        t.wait_ge(g.ss.h if eng == "s" else g.sv.h, cnt)
                    else:
                        bank_g[bank].wait(t)
                    mm = None
                    for k in range(JT):
                        mm = t.matmul(g.pb[bank][:, 0:LC],
                                      ffg[:, k * 2 * DM + m * 128:k * 2 * DM + (m + 1) * 128],
                                      ln2[:, k * LC:(k + 1) * LC],
                                      start=(k == 0), stop=(k == JT - 1))
                    fgm.append(g.st.inc(mm))
                    _ffg_ev(m)

            def _ffg_ev(m):
                bank = 2 + (m % 2)
                if m < JT:
                    def _ev(s, m=m):
                        s.wait_ge(g.st.h, fgm[m])
                        ev = s.activation(out=sg2[:, m * LC:(m + 1) * LC],
                                          in_=g.pb[bank][:, 0:LC], func=AF.Silu,
                                          bias=g.ffgb[:, m:m + 1])
                        fge.append(("s", g.ss.inc(ev)))
                    b.scalar(_ev)
                else:
                    def _ev(v, m=m):
                        mj = m - JT
                        v.wait_ge(g.st.h, fgm[m])
                        eng, cnt = fge[mj]
                        v.wait_ge(g.ss.h, cnt)
                        ev = v.scalar_tensor_tensor(out=ff2[:, mj * LC:(mj + 1) * LC],
                                                    in0=g.pb[bank][:, 0:LC],
                                                    scalar=g.ffgb[:, JT + mj:JT + mj + 1],
                                                    in1=sg2[:, mj * LC:(mj + 1) * LC],
                                                    op0=OP.add, op1=OP.mult)
                        fge.append(("v", g.sv.inc(ev)))
                    b.vector(_ev)
            b.tensor(_ffg)
            e0, c0 = fge[-2]
            e1, c1 = fge[-1]
            bank_g[2].set(g.ss if e0 == "s" else g.sv, c0)
            bank_g[3].set(g.ss if e1 == "s" else g.sv, c1)
            ffn = c1

            fpm = []
            sv_ = []

            def _ffp(t, ci=ci):
                t.wait_ge(g.sv.h, ffn)
                for mo in range(JT):
                    bank = 4 + (mo % 2)
                    if mo >= 2:
                        t.wait_ge(g.sv.h, sv_[mo - 2])
                    else:
                        bank_g[bank].wait(t)
                    mm = None
                    for k in range(JT):
                        mm = t.matmul(g.pb[bank][:, 0:LC],
                                      ffp[:, k * DM + mo * 128:k * DM + (mo + 1) * 128],
                                      ff2[:, k * LC:(k + 1) * LC],
                                      start=(k == 0), stop=(k == JT - 1))
                    fpm.append(g.st.inc(mm))
                    _s_one(mo)

            def _s_one(mo):
                def _sv(v, mo=mo):
                    bank = 4 + (mo % 2)
                    v.wait_ge(g.st.h, fpm[mo])
                    sv2 = v.scalar_tensor_tensor(out=xb[:, mo * LC:(mo + 1) * LC],
                                                 in0=g.pb[bank][:, 0:LC],
                                                 scalar=g.ffpb[:, mo:mo + 1],
                                                 in1=xb[:, mo * LC:(mo + 1) * LC],
                                                 op0=OP.add, op1=OP.add)
                    sv_.append(g.sv.inc(sv2))
                b.vector(_sv)
            b.tensor(_ffp)
            bank_g[4].set(g.sv, sv_[-2])
            bank_g[5].set(g.sv, sv_[-1])
            sn = sv_[-1]

            def _sst(sync, bb=bb, cc=cc):
                sync.wait_ge(g.sv.h, sn)
                g.sy.inc(sync.dma_start(
                    out=g.h_s[bb][:, cc * LC:(cc + 1) * LC].rearrange("(j p) t -> p j t", p=128),
                    in_=xf2[:, :].rearrange("p (j t) -> p j t", t=LC)), 16)
            b.sync(_sst)
            sst.append(g.sy.n)

            chunk_done.append(g.sv.n)
            act_g.add(g.st, g.st.n)
            act_g.add(g.sy, sst[-1])
            act_g.add(g.ss, g.ss.n)


def emit_p3(g):
    nc, b = g.nc, g.block
    with ExitStack() as c2:
        E = c2.enter_context
        xf3 = [E(nc.sbuf_tensor(f"xf3_{i}", [128, JT * LC], F32)) for i in range(2)]
        sf3 = [E(nc.sbuf_tensor(f"sf3_{i}", [128, JT * LC], F32)) for i in range(2)]
        gf3 = [E(nc.sbuf_tensor(f"gf3_{i}", [128, JT * LC], BF16)) for i in range(2)]
        o3 = [E(nc.sbuf_tensor(f"o3_{i}", [128, JT * LC], F32)) for i in range(2)]

        chunks = [(bb, cc) for bb in range(BC) for cc in range(NCH)]
        lds = []
        ov = []
        st3 = []

        for ci, (bb, cc) in enumerate(chunks):
            def _ld(sync, ci=ci, bb=bb, cc=cc):
                if ci >= 2:
                    sync.wait_ge(g.sv.h, ov[ci - 2])
                col = cc * LC
                g.sy.inc(sync.dma_start(
                    out=xf3[ci % 2][:, :].rearrange("p (j t) -> p j t", t=LC),
                    in_=g.xt[bb][:, col:col + LC].rearrange("(j p) t -> p j t", p=128)), 16)
                g.sy.inc(sync.dma_start(
                    out=sf3[ci % 2][:, :].rearrange("p (j t) -> p j t", t=LC),
                    in_=g.h_s[bb][:, col:col + LC].rearrange("(j p) t -> p j t", p=128)), 16)
                g.sy.inc(sync.dma_start(
                    out=gf3[ci % 2][:, :].rearrange("p (j t) -> p j t", t=LC),
                    in_=g.h_g[bb][:, col:col + LC].rearrange("(j p) t -> p j t", p=128)), 16)
            b.sync(_ld)
            lds.append(g.sy.n)

            def _ov(v, ci=ci):
                v.wait_ge(g.sy.h, lds[ci])
                if ci >= 2:
                    v.wait_ge(g.sy.h, st3[ci - 2])
                last = None
                for j in range(JT):
                    sl = o3[ci % 2][:, j * LC:(j + 1) * LC]
                    v.tensor_tensor(out=sl, in0=sf3[ci % 2][:, j * LC:(j + 1) * LC],
                                    in1=gf3[ci % 2][:, j * LC:(j + 1) * LC], op=OP.mult)
                    last = v.tensor_tensor(out=sl, in0=sl,
                                           in1=xf3[ci % 2][:, j * LC:(j + 1) * LC], op=OP.add)
                g.sv.inc(last)
            b.vector(_ov)
            ov.append(g.sv.n)
            ovn = g.sv.n

            def _st(sync, ci=ci, bb=bb, cc=cc):
                sync.wait_ge(g.sv.h, ovn)
                col = cc * LC
                g.sy.inc(sync.dma_start(
                    out=g.out_cm[bb][:, col:col + LC].rearrange("(j p) t -> p j t", p=128),
                    in_=o3[ci % 2][:, :].rearrange("p (j t) -> p j t", t=LC)), 16)
            b.sync(_st)
            st3.append(g.sy.n)


def _prep_inputs(kw):
    bf = ml_dtypes.bfloat16
    f32 = np.float32

    def T(a):
        return np.ascontiguousarray(np.asarray(a, f32).T)

    x = np.asarray(kw["L0_attn_x"], f32)
    com = {}
    com["w_inproj"] = T(kw["L0_attn_in_proj_w"])
    com["w_xproj"] = T(kw["L0_attn_x_proj_w"])
    com["w_dtw"] = T(kw["L0_attn_dt_w"])
    com["w_outw"] = T(kw["L0_attn_out_w"])
    com["w_ffg"] = T(kw["L0_attn_ffg_w"])
    com["w_ffp"] = T(kw["L0_attn_ffp_w"])
    com["w_router"] = T(kw["L0_ffn_router_w"])
    com["w_sg"] = T(kw["L0_ffn_sg"])
    com["w_su"] = T(kw["L0_ffn_su"])
    com["w_sd"] = T(kw["L0_ffn_sd"])
    com["w_eg"] = np.ascontiguousarray(np.asarray(kw["L0_ffn_eg"], f32).transpose(0, 2, 1))
    com["w_eu"] = np.ascontiguousarray(np.asarray(kw["L0_ffn_eu"], f32).transpose(0, 2, 1))
    com["w_ed"] = np.ascontiguousarray(np.asarray(kw["L0_ffn_ed"], f32).transpose(0, 2, 1))

    def vtile(v, k):
        v = np.asarray(v, f32).reshape(k, 128)
        return np.ascontiguousarray(v.T)

    com["v_n1g"] = vtile(kw["L0_attn_n1_g"], JT)
    com["v_n1b"] = vtile(kw["L0_attn_n1_b"], JT)
    com["v_n2g"] = vtile(kw["L0_attn_n2_g"], JT)
    com["v_n2b"] = vtile(kw["L0_attn_n2_b"], JT)
    cw = np.asarray(kw["L0_attn_conv_w"], f32).reshape(DI, 4)
    com["v_convw"] = np.ascontiguousarray(
        cw.reshape(IT, 128, 4).transpose(1, 0, 2).reshape(128, IT * 4))
    com["v_convb"] = vtile(kw["L0_attn_conv_b"], IT)
    com["v_dtb"] = vtile(kw["L0_attn_dt_b"], IT)
    com["v_D"] = vtile(kw["L0_attn_D"], IT)
    com["v_ffgb"] = vtile(kw["L0_attn_ffg_b"], 2 * JT)
    com["v_ffpb"] = vtile(kw["L0_attn_ffp_b"], JT)
    al = np.asarray(kw["L0_attn_A_log"], f32)
    com["v_Alog"] = np.ascontiguousarray(
        al.reshape(IT, 128, NS).transpose(1, 0, 2).reshape(128, IT * NS))
    com["c_id"] = np.eye(128, dtype=f32)
    com["c_idf"] = np.eye(128, dtype=f32)
    oh16 = np.zeros((16, 16 * 128), f32)
    for n in range(16):
        oh16[n, n * 128:(n + 1) * 128] = 1
    com["c_oh16"] = oh16
    oh8 = np.zeros((8, 8 * 128), f32)
    for e in range(8):
        oh8[e, e * 128:(e + 1) * 128] = 1
    com["c_oh8"] = oh8

    in_maps = []
    for c in range(8):
        m = dict(com)
        m["xt"] = np.ascontiguousarray(x[2 * c:2 * c + 2].transpose(0, 2, 1))
        in_maps.append(m)
    return in_maps


def kernel(**kw):
    in_maps = _prep_inputs(kw)
    nc = build()
    res = run_bass_kernel_spmd(nc, in_maps, core_ids=list(range(8)))
    out = np.empty((16, L, DM), np.float32)
    for c in range(8):
        o = np.asarray(res.results[c]["out_cm"])
        out[2 * c:2 * c + 2] = o.transpose(0, 2, 1)
    return out
